# version 01
# baseline (speedup 1.0000x reference)
import contextlib
import numpy as np
import ml_dtypes
import concourse.bass as bass
import concourse.mybir as mybir
from concourse.bass_utils import run_bass_kernel_spmd

F32 = mybir.dt.float32
BF16 = mybir.dt.bfloat16
AF = mybir.ActivationFunctionType
ALU = mybir.AluOpType
AX = mybir.AxisListType.X

NCORES = 8
T = 2048
NT = T // 128
D = 1024
NC = D // 128
PAD = 16
TPAD = T + PAD

ENGINES = ("sync", "scalar", "vector", "tensor", "gpsimd")
DMA_QUEUES = ("sync", "scalar", "gpsimd")

NEG = -3e38


class Prog:

    def __init__(self, nc, stack):
        self.nc = nc
        self.stack = stack
        self.ops = {e: [] for e in ENGINES}
        self.cnt = {e: 0 for e in ENGINES}
        self.dcnt = {q: 0 for q in DMA_QUEUES}
        self.psem = {e: stack.enter_context(nc.semaphore(f"prog_{e}")) for e in ENGINES}
        self.NDQ = 4
        self.dsem = {q: [stack.enter_context(nc.semaphore(f"dq_{q}{i}"))
                         for i in range(self.NDQ)] for q in DMA_QUEUES}
        self.waited = {e: {} for e in ENGINES}

    def _mkwaits(self, eng, deps):
        need = {}
        for dp in deps:
            if dp is None:
                continue
            kind, src, val = dp
            if kind == "e" and src == eng and eng == "tensor":
                continue
            if kind == "d":
                i = val - 1
                key = (kind, src, i % self.NDQ)
                val = 16 * (i // self.NDQ + 1)
            else:
                key = (kind, src)
            if self.waited[eng].get(key, 0) >= val:
                continue
            if need.get(key, 0) < val:
                need[key] = val
        for key, val in sorted(need.items()):
            self.waited[eng][key] = val
            if key[0] == "e":
                sem = self.psem[key[1]]
            else:
                sem = self.dsem[key[1]][key[2]]
            self.ops[eng].append(lambda E, sem=sem, v=val: E.wait_ge(sem, v))

    def op(self, eng, fn, deps=(), inc=True):
        if eng != "tensor":
            inc = True
            deps = tuple(deps) + (("e", eng, self.cnt[eng]),)
        self._mkwaits(eng, deps)
        if inc:
            self.cnt[eng] += 1
            c = self.cnt[eng]
            sem = self.psem[eng]
            self.ops[eng].append(lambda E, fn=fn, sem=sem: fn(E).then_inc(sem, 1))
            return ("e", eng, c)
        self.ops[eng].append(lambda E, fn=fn: fn(E))
        return ("e", eng, self.cnt[eng])

    def dma(self, q, out, in_, deps=(), **kw):
        i = self.dcnt[q]
        deps = tuple(deps) + (("e", q, self.cnt[q]),)
        if i >= self.NDQ:
            deps = deps + (("d", q, i - self.NDQ + 1),)
        self._mkwaits(q, deps)
        self.dcnt[q] += 1
        sem = self.dsem[q][i % self.NDQ]
        self.ops[q].append(
            lambda E, out=out, in_=in_, kw=kw, sem=sem:
            E.dma_start(out=out, in_=in_, **kw).then_inc(sem, 16))
        return ("d", q, self.dcnt[q])

    def barrier(self):
        toks = [("e", e, self.cnt[e]) for e in ENGINES]
        for q in DMA_QUEUES:
            for i in range(max(0, self.dcnt[q] - self.NDQ + 1), self.dcnt[q] + 1):
                if i >= 1:
                    toks.append(("d", q, i))
        for e in ENGINES:
            self._mkwaits(e, toks)

    def emit(self):
        nc = self.nc
        with nc.Block() as block:
            @block.sync
            def _(E):
                for f in self.ops["sync"]:
                    f(E)

            @block.scalar
            def _(E):
                for f in self.ops["scalar"]:
                    f(E)

            @block.vector
            def _(E):
                for f in self.ops["vector"]:
                    f(E)

            @block.tensor
            def _(E):
                for f in self.ops["tensor"]:
                    f(E)

            @block.gpsimd
            def _(E):
                for f in self.ops["gpsimd"]:
                    f(E)


def mm_group(p, psum_ap, pairs, deps=()):
    n = len(pairs)
    tok = None
    for i, (l, r) in enumerate(pairs):
        last = i == n - 1
        tok = p.op("tensor",
                   (lambda E, l=l, r=r, s=(i == 0), e=last:
                    E.matmul(psum_ap, l, r, start=s, stop=e)),
                   deps=deps if i == 0 else (), inc=last)
    return tok


class Rot:

    def __init__(self, items):
        self.items = list(items)
        self.busy = [None] * len(self.items)
        self.i = 0

    def get(self):
        j = self.i % len(self.items)
        self.i += 1
        return self.items[j], self.busy[j], j

    def set_busy(self, j, tok):
        self.busy[j] = tok



def make_rope_const():
    hd = np.arange(0, 64, 2, dtype=np.float64)
    inv_freq = 1.0 / 10000.0 ** (hd / 64.0)
    tpos = np.arange(T, dtype=np.float64)
    fr = tpos[:, None] * inv_freq[None, :]
    freqs = np.concatenate([fr, fr], axis=-1)
    base = (hd + 0.4 * 64) / (1.4 * 64)
    power = (tpos - T // 2) / 256.0
    xsc = base[None, :] ** power[:, None]
    xsc = np.concatenate([xsc, xsc], axis=-1)
    return np.concatenate(
        [np.cos(freqs), np.sin(freqs), xsc, 1.0 / xsc], axis=1).astype(np.float32)


def prep_host(inp):
    d = {}

    def add(name, arr):
        d[name] = np.ascontiguousarray(np.asarray(arr, dtype=np.float32))

    add("c_rope", make_rope_const())
    add("l0_wqT", np.asarray(inp["L0_attn_wq"]).T)
    add("l0_wkT", np.asarray(inp["L0_attn_wk"]).T)
    add("l0_wvT", np.asarray(inp["L0_attn_wv"]).T)
    add("l0_fwq", inp["L0_ffn_Wq"])
    add("l0_bq_col", np.asarray(inp["L0_ffn_bq"]).reshape(16, 128).T)
    add("l0_bq_row", np.asarray(inp["L0_ffn_bq"]).reshape(1, 2048))
    add("l0_gam_col", np.asarray(inp["L0_ffn_bn_gamma"]).reshape(2, 128).T)
    add("l0_bet_col", np.asarray(inp["L0_ffn_bn_beta"]).reshape(2, 128).T)
    add("l0_sub0", inp["L0_ffn_sub_key0"])
    add("l0_sub1", inp["L0_ffn_sub_key1"])
    add("l0_downT", np.asarray(inp["L0_ffn_expert_down"]).T)
    add("l0_up", inp["L0_ffn_expert_up"])
    add("l1_wq", inp["L1_attn_Wq"])
    add("l1_wk", inp["L1_attn_Wk"])
    add("l1_wv", inp["L1_attn_Wv"])
    add("l1_wo", inp["L1_attn_Wo"])
    add("l1_lng_row", np.asarray(inp["L1_attn_ln_g"]).reshape(1, 1024))
    add("l1_lnb_row", np.asarray(inp["L1_attn_ln_b"]).reshape(1, 1024))
    add("l1_rwT", np.asarray(inp["L1_ffn_router_w"]).T)
    add("l1_w1T", np.asarray(inp["L1_ffn_w1"]).transpose(0, 2, 1))
    add("l1_w2T", np.asarray(inp["L1_ffn_w2"]).transpose(0, 2, 1))
    add("l2_wq", inp["L2_attn_wq"]); add("l2_wk", inp["L2_attn_wk"])
    add("l2_wv", inp["L2_attn_wv"]); add("l2_wbeta", inp["L2_attn_wbeta"])
    add("l2_bq_row", np.asarray(inp["L2_attn_bq"]).reshape(1, 1024))
    add("l2_bq_col", np.asarray(inp["L2_attn_bq"]).reshape(8, 128).T)
    add("l2_bk_row", np.asarray(inp["L2_attn_bk"]).reshape(1, 1024))
    add("l2_bv_row", np.asarray(inp["L2_attn_bv"]).reshape(1, 1024))
    add("l2_bbeta_row", np.asarray(inp["L2_attn_bbeta"]).reshape(1, 16))
    add("l2_wo", inp["L2_attn_wo"])
    add("l2_bo_row", np.asarray(inp["L2_attn_bo"]).reshape(1, 1024))
    add("l2_w1", inp["L2_attn_w1"])
    add("l2_b1_col", np.asarray(inp["L2_attn_b1"]).reshape(32, 128).T)
    add("l2_w2", inp["L2_attn_w2"])
    add("l2_b2_row", np.asarray(inp["L2_attn_b2"]).reshape(1, 1024))
    add("l2_ln1_row", np.asarray(inp["L2_attn_ln1_scale"]).reshape(1, 1024))
    add("l2_ln2_row", np.asarray(inp["L2_attn_ln2_scale"]).reshape(1, 1024))
    add("l2f_wqkv", inp["L2_ffn_wqkv"])
    add("l2f_qs_row", np.tile(np.asarray(inp["L2_ffn_q_scale"]), (1, 8)))
    add("l2f_ks_row", np.tile(np.asarray(inp["L2_ffn_k_scale"]), (1, 8)))
    add("l2f_wo", inp["L2_ffn_wo"])
    add("l2f_lng_col", np.asarray(inp["L2_ffn_ln_g"]).reshape(2, 8, 128).transpose(2, 0, 1).reshape(128, 16))
    add("l2f_lnb_col", np.asarray(inp["L2_ffn_ln_b"]).reshape(2, 8, 128).transpose(2, 0, 1).reshape(128, 16))
    add("l2f_w1", inp["L2_ffn_w1"])
    add("l2f_b1_row", np.asarray(inp["L2_ffn_b1"]).reshape(2, 1, 5460))
    add("l2f_w2", inp["L2_ffn_w2"])
    add("l2f_b2_row", np.asarray(inp["L2_ffn_b2"]).reshape(2, 1, 1024))
    add("l3_wattn", inp["L3_attn_W_attn"])
    add("l3_battn_col", np.asarray(inp["L3_attn_b_attn"]).reshape(24, 128).T)
    add("l3_battn_row", np.asarray(inp["L3_attn_b_attn"]).reshape(1, 3072))
    add("l3_wproj", inp["L3_attn_W_proj"])
    add("l3_bproj_row", np.asarray(inp["L3_attn_b_proj"]).reshape(1, 1024))
    add("l3_wg", inp["L3_ffn_Wg"])
    add("l3_w1", inp["L3_ffn_w1"])
    add("l3_w2", inp["L3_ffn_w2"])
    return d



class Core:
    def __init__(self, p, stack):
        nc = p.nc
        sb = lambda name, shape, dt: stack.enter_context(nc.sbuf_tensor(name, shape, dt))
        self.p = p
        self.x = sb("x_res", [128, NT, D], F32)
        self.stats = sb("stats", [128, NT, 8], F32)
        self.ident_bf = sb("ident_bf", [128, 128], BF16)
        self.ident_f = sb("ident_f", [128, 128], F32)
        self.ones_col_f = sb("ones_col_f", [128, 1], F32)
        self.ones_col_bf = sb("ones_col_bf", [128, 1], BF16)
        self.ones_row_bf = sb("ones_row_bf", [1, 128], BF16)
        self.ones_row_f = sb("ones_row_f", [1, 128], F32)
        self.selA = sb("selA", [128, 1], F32)
        self.selB = sb("selB", [128, 1], F32)
        self.mask4 = sb("mask4", [128, 4, 512], BF16)
        self.mask_tril_add = sb("mask_trl", [128, 128], BF16)
        self.cst = sb("cst", [128, 4], F32)
        self.sc_f = sb("scratch_f", [128, 1024], F32)
        self.sc_f2 = sb("scratch_f2", [128, 512], F32)

    def init_consts(self, p, x_dram, x_load_q="sync"):
        c = self
        g = lambda fn, deps=(): p.op("gpsimd", fn, deps=deps)
        g(lambda E: E.memset(c.ident_bf[:, :], 0.0))
        g(lambda E: E.affine_select(out=c.ident_bf[:, :], in_=c.ident_bf[:, :],
                                    compare_op=ALU.not_equal, fill=1.0, base=0,
                                    pattern=[[-1, 128]], channel_multiplier=1))
        g(lambda E: E.memset(c.ident_f[:, :], 0.0))
        g(lambda E: E.affine_select(out=c.ident_f[:, :], in_=c.ident_f[:, :],
                                    compare_op=ALU.not_equal, fill=1.0, base=0,
                                    pattern=[[-1, 128]], channel_multiplier=1))
        g(lambda E: E.memset(c.ones_col_f[:, :], 1.0))
        g(lambda E: E.memset(c.ones_col_bf[:, :], 1.0))
        g(lambda E: E.memset(c.ones_row_bf[:, :], 1.0))
        g(lambda E: E.memset(c.ones_row_f[:, :], 1.0))
        g(lambda E: E.memset(c.selA[:, :], 0.0))
        g(lambda E: E.memset(c.selB[:, :], 0.0))
        g(lambda E: E.memset(c.selA[0:64, :], 1.0))
        g(lambda E: E.memset(c.selB[64:128, :], 1.0))
        g(lambda E: E.memset(c.cst[:, 0:1], 1e-5))
        g(lambda E: E.memset(c.cst[:, 1:2], 1e-6))
        g(lambda E: E.memset(c.cst[:, 2:3], 1e-12))
        g(lambda E: E.memset(c.mask4[:, :, :], 0.0))
        for k in range(4):
            g(lambda E, k=k: E.affine_select(
                out=c.mask4[:, k, :], in_=c.mask4[:, k, :],
                compare_op=ALU.is_ge, fill=NEG, base=-128 * k,
                pattern=[[1, 512]], channel_multiplier=-1))
        g(lambda E: E.memset(c.mask_tril_add[:, :], 0.0))
        g(lambda E: E.affine_select(
            out=c.mask_tril_add[:, :], in_=c.mask_tril_add[:, :],
            compare_op=ALU.is_ge, fill=NEG, base=0,
            pattern=[[-1, 128]], channel_multiplier=1))
        for ti in range(NT):
            p.dma(x_load_q, c.x[:, ti, :],
                  x_dram.rearrange("(n p) d -> n p d", p=128)[ti])
        p.barrier()


def emit_ln(p, c, extra_rstd2=False, dst=None):
    dst = dst if dst is not None else c.xn_t
    for ti in range(NT):
        x = c.x[:, ti, :]
        st = c.stats[:, ti, :]
        t1 = p.op("vector", lambda E, x=x, st=st: E.tensor_reduce(
            out=st[:, 3:4], in_=x, axis=AX, op=ALU.add))
        t2 = p.op("vector", lambda E, x=x, st=st: E.scalar_tensor_tensor(
            out=c.sc_f[:, 0:1024], in0=x, scalar=1.0, in1=x,
            op0=ALU.mult, op1=ALU.mult, accum_out=st[:, 4:5]))
        t3 = p.op("vector", lambda E, st=st: E.tensor_scalar_mul(
            out=st[:, 0:1], in0=st[:, 3:4], scalar1=1.0 / 1024.0), deps=(t1,))
        t4 = p.op("vector", lambda E, st=st: E.tensor_scalar(
            out=st[:, 5:6], in0=st[:, 0:1], scalar1=st[:, 0:1], scalar2=None,
            op0=ALU.mult), deps=(t3,))
        t5 = p.op("vector", lambda E, st=st: E.tensor_scalar(
            out=st[:, 6:7], in0=st[:, 4:5], scalar1=1.0 / 1024.0, scalar2=st[:, 5:6],
            op0=ALU.mult, op1=ALU.subtract), deps=(t2, t4))
        t6 = p.op("scalar", lambda E, st=st: E.activation(
            out=st[:, 7:8], in_=st[:, 6:7], func=AF.Sqrt, bias=c.cst[:, 0:1]), deps=(t5,))
        t7 = p.op("vector", lambda E, st=st: E.reciprocal(
            out=st[:, 1:2], in_=st[:, 7:8]), deps=(t6,))
        p.op("vector", lambda E, x=x, st=st, ti=ti, dst=dst: E.tensor_scalar(
            out=dst[:, ti, :], in0=x, scalar1=st[:, 0:1], scalar2=st[:, 1:2],
            op0=ALU.subtract, op1=ALU.mult), deps=(t7,))
        if extra_rstd2:
            ta = p.op("vector", lambda E, st=st: E.tensor_scalar(
                out=st[:, 5:6], in0=st[:, 1:2], scalar1=st[:, 1:2], scalar2=st[:, 6:7],
                op0=ALU.mult, op1=ALU.mult), deps=(t7,))
            tb = p.op("scalar", lambda E, st=st: E.activation(
                out=st[:, 7:8], in_=st[:, 5:6], func=AF.Sqrt, bias=c.cst[:, 0:1]), deps=(ta,))
            p.op("vector", lambda E, st=st: E.reciprocal(
                out=st[:, 2:3], in_=st[:, 7:8]), deps=(tb,))


def emit_t2ch(p, c, src_t, dst_ch, ps_pair, tiles=None, chunks=None, col0=PAD):
    rot = Rot(ps_pair)
    tiles = range(NT) if tiles is None else tiles
    chunks = range(NC) if chunks is None else chunks
    for ti in tiles:
        for ch in chunks:
            ps, busy, j = rot.get()
            tt = p.op("tensor", lambda E, ti=ti, ch=ch, ps=ps: E.transpose(
                ps[:, :], src_t[:, ti, ch * 128:(ch + 1) * 128], c.ident_bf[:, :]),
                deps=(busy,))
            te = p.op("vector", lambda E, ti=ti, ch=ch, ps=ps: E.tensor_copy(
                out=dst_ch[:, ch, col0 + ti * 128:col0 + (ti + 1) * 128], in_=ps[:, :]),
                deps=(tt,))
            rot.set_busy(j, te)



def emit_l0_stats(p, c, W, stack, A1, B1):
    nc = p.nc
    TT = 16384
    xn_dram = nc.dram_tensor("l0_xn_scratch", [TT, D], BF16)
    xfull = W["x_full"].rearrange("(n p) d -> n p d", p=128)
    xnd = xn_dram.rearrange("(n p) d -> n p d", p=128)
    with contextlib.ExitStack() as st:
        sb2 = lambda n, s, d: st.enter_context(nc.sbuf_tensor(n, s, d))
        G_sb = sb2("l0_G", [128, NC, D], F32)
        srow = sb2("l0_srow", [1, D], F32)
        rowsA = sb2("l0_rowsA", [1, 2048], F32)
        rowsB = sb2("l0_rowsB", [1, 2048], F32)
        rowsC = sb2("l0_rowsC", [1, 2048], F32)
        r256 = sb2("l0_r256", [1, 512], F32)
        gst = sb2("l0_gst", [128, 8], F32)
        with contextlib.ExitStack() as st3:
            xng = st3.enter_context(nc.sbuf_tensor("l0_xng", [128, 8, D], BF16))
            stP = st3.enter_context(contextlib.ExitStack())
            xg = stP.enter_context(nc.sbuf_tensor("l0_xg", [128, 8, D], F32))
            psS0 = stP.enter_context(nc.psum_tensor("l0_psS0", [1, 512], F32))
            psS1 = stP.enter_context(nc.psum_tensor("l0_psS1", [1, 512], F32))
            last_use = None
            last_sv = None
            last_wbs = []
            for ch in range(16):
                ld = p.dma("sync", xg[:, :, :],
                           xfull.rearrange("n p d -> p n d")[:, ch * 8:(ch + 1) * 8, :],
                           deps=tuple([last_use, last_sv] + last_wbs[-4:]))
                for t in range(8):
                    x = xg[:, t, :]
                    t1 = p.op("vector", lambda E, x=x: E.tensor_reduce(
                        out=gst[:, 0:1], in_=x, axis=AX, op=ALU.add), deps=(ld,))
                    t2 = p.op("vector", lambda E, x=x: E.scalar_tensor_tensor(
                        out=c.sc_f[:, 0:1024], in0=x, scalar=1.0, in1=x,
                        op0=ALU.mult, op1=ALU.mult, accum_out=gst[:, 1:2]))
                    t3 = p.op("vector", lambda E: E.tensor_scalar_mul(
                        out=gst[:, 2:3], in0=gst[:, 0:1], scalar1=1.0 / 1024.0))
                    t4 = p.op("vector", lambda E: E.tensor_scalar(
                        out=gst[:, 3:4], in0=gst[:, 2:3], scalar1=gst[:, 2:3],
                        scalar2=None, op0=ALU.mult))
                    t5 = p.op("vector", lambda E: E.tensor_scalar(
                        out=gst[:, 4:5], in0=gst[:, 1:2], scalar1=1.0 / 1024.0,
                        scalar2=gst[:, 3:4], op0=ALU.mult, op1=ALU.subtract))
                    t6 = p.op("scalar", lambda E: E.activation(
                        out=gst[:, 5:6], in_=gst[:, 4:5], func=AF.Sqrt,
                        bias=c.cst[:, 0:1]), deps=(t5,))
                    t7 = p.op("vector", lambda E: E.reciprocal(
                        out=gst[:, 6:7], in_=gst[:, 5:6]), deps=(t6,))
                    last_use = p.op("vector", lambda E, x=x, t=t: E.tensor_scalar(
                        out=xng[:, t, :], in0=x, scalar1=gst[:, 2:3],
                        scalar2=gst[:, 6:7], op0=ALU.subtract, op1=ALU.mult))
                    last_wbs.append(p.dma("sync", xnd[ch * 8 + t], xng[:, t, :],
                                          deps=(last_use,)))
                    for half, bank in ((0, psS0), (1, psS1)):
                        last_sv = p.op("tensor",
                                       lambda E, t=t, half=half, bank=bank,
                                       s=(ch == 0 and t == 0), e=(ch == 15 and t == 7):
                                       E.matmul(bank[0:1, :], c.ones_col_bf[:, :],
                                                xng[:, t, half * 512:(half + 1) * 512],
                                                start=s, stop=e, skip_group_check=True),
                                       deps=(last_use,), inc=(t == 7 and half == 1))
            p.barrier()
            cv0 = p.op("vector", lambda E: E.tensor_copy(out=srow[0:1, 0:512],
                                                         in_=psS0[0:1, :]))
            cv1 = p.op("vector", lambda E: E.tensor_copy(out=srow[0:1, 512:1024],
                                                         in_=psS1[0:1, :]))
            p.barrier()
            stP.close()
            for h2 in range(2):
                with contextlib.ExitStack() as st4:
                    psG = [st4.enter_context(
                        nc.psum_tensor(f"l0_psG{h2}_{i}", [128, 512], F32))
                        for i in range(8)]
                    last_mm = None
                    for ch in range(16):
                        ld = p.dma("sync", xng[:, :, :],
                                   xnd.rearrange("n p d -> p n d")[:, ch * 8:(ch + 1) * 8, :],
                                   deps=(last_mm,))
                        for cr in range(8):
                            for t in range(8):
                                last_mm = p.op(
                                    "tensor",
                                    lambda E, cr=cr, t=t,
                                    s=(ch == 0 and t == 0), e=(ch == 15 and t == 7):
                                    E.matmul(psG[cr][:, :],
                                             xng[:, t, cr * 128:(cr + 1) * 128],
                                             xng[:, t, h2 * 512:(h2 + 1) * 512],
                                             start=s, stop=e, skip_group_check=True),
                                    deps=(ld,) if (cr == 0 and t == 0) else (),
                                    inc=(cr == 7 and t == 7))
                    p.barrier()
                    for cr in range(8):
                        p.op("vector", lambda E, cr=cr, h2=h2: E.tensor_copy(
                            out=G_sb[:, cr, h2 * 512:(h2 + 1) * 512],
                            in_=psG[cr][:, :]))
                    p.barrier()
        with contextlib.ExitStack() as st3:
            sb3 = lambda n, s, d: st3.enter_context(nc.sbuf_tensor(n, s, d))
            G_bf = sb3("l0_Gbf", [128, NC, D], BF16)
            wq_sb = sb3("l0_wq_sb", [128, NC, 2048], BF16)
            bq_row = sb3("l0_bqrow", [1, 2048], F32)
            scol = sb3("l0_scol", [128, NC], BF16)
            colt = sb3("l0_colt", [128, 8], F32)
            psH = [st3.enter_context(nc.psum_tensor(f"l0_psH{i}", [128, 512], F32))
                   for i in range(2)]
            psD = [st3.enter_context(nc.psum_tensor(f"l0_psD{i}", [128, 512], F32))
                   for i in range(4)]
            psT = st3.enter_context(nc.psum_tensor("l0_psT", [128, 128], F32))
            lw = p.dma("gpsimd", wq_sb[:, :, :],
                       W["l0_fwq"].rearrange("(c p) n -> p c n", p=128))
            lb = p.dma("sync", bq_row[:, :], W["l0_bq_row"][:, :])
            for cr in range(8):
                p.op("vector", lambda E, cr=cr: E.tensor_copy(
                    out=G_bf[:, cr, :], in_=G_sb[:, cr, :]))
            prevev = None
            for kc in range(8):
                tt = p.op("tensor", lambda E, kc=kc: E.matmul(
                    psT[:, 0:1], srow[0:1, kc * 128:(kc + 1) * 128],
                    c.ones_row_f[0:1, 0:1], start=True, stop=True),
                    deps=(cv0, cv1, prevev), inc=True)
                prevev = p.op("vector", lambda E, kc=kc: E.tensor_copy(
                    out=scol[:, kc:kc + 1], in_=psT[:, 0:1]), deps=(tt,))
            p.barrier()
            prevcp = None
            for jg in range(4):
                mm = mm_group(p, psH[0][0:1, :],
                              [(scol[:, kc:kc + 1], wq_sb[:, kc, jg * 512:(jg + 1) * 512])
                               for kc in range(8)], deps=(lw, prevcp))
                prevcp = p.op("vector", lambda E, jg=jg: E.tensor_copy(
                    out=rowsA[0:1, jg * 512:(jg + 1) * 512], in_=psH[0][0:1, :]),
                    deps=(mm,))
            p.barrier()
            rot = Rot(psH)
            for jg in range(4):
                for dc in range(8):
                    bank, busy, j = rot.get()
                    mm = mm_group(p, bank[:, :],
                                  [(G_bf[:, ec, dc * 128:(dc + 1) * 128],
                                    wq_sb[:, ec, jg * 512:(jg + 1) * 512])
                                   for ec in range(8)], deps=(busy,))
                    tm = p.op("vector", lambda E, dc=dc, jg=jg, bank=bank:
                              E.tensor_tensor(out=c.sc_f[:, 0:512], in0=bank[:, :],
                                              in1=wq_sb[:, dc, jg * 512:(jg + 1) * 512],
                                              op=ALU.mult), deps=(mm,))
                    dm = p.op("tensor", lambda E, jg=jg, s=(dc == 0), e=(dc == 7):
                              E.matmul(psD[jg][0:1, :], c.ones_col_f[:, :],
                                       c.sc_f[:, 0:512], start=s, stop=e,
                                       skip_group_check=True), deps=(tm,),
                              inc=(dc == 7))
                    rot.set_busy(j, dm)
            p.barrier()
            for jg in range(4):
                p.op("vector", lambda E, jg=jg: E.tensor_copy(
                    out=rowsB[0:1, jg * 512:(jg + 1) * 512], in_=psD[jg][0:1, :]))
            p.barrier()
            p.op("vector", lambda E: E.tensor_scalar_mul(
                out=rowsC[0:1, :], in0=rowsA[0:1, :], scalar1=2.0))
            p.op("vector", lambda E: E.scalar_tensor_tensor(
                out=rowsC[0:1, :], in0=bq_row[0:1, :], scalar=16384.0,
                in1=rowsC[0:1, :], op0=ALU.mult, op1=ALU.add), deps=(lb,))
            p.op("vector", lambda E: E.tensor_tensor(
                out=rowsC[0:1, :], in0=rowsC[0:1, :], in1=bq_row[0:1, :],
                op=ALU.mult))
            p.op("vector", lambda E: E.tensor_tensor(
                out=rowsB[0:1, :], in0=rowsB[0:1, :], in1=rowsC[0:1, :],
                op=ALU.add))
            p.op("vector", lambda E: E.scalar_tensor_tensor(
                out=rowsA[0:1, :], in0=bq_row[0:1, :], scalar=16384.0,
                in1=rowsA[0:1, :], op0=ALU.mult, op1=ALU.add))
            p.op("vector", lambda E: E.tensor_reduce(
                out=r256[0:1, 0:256],
                in_=rowsA[0:1, :].rearrange("o (h c) -> o c h", h=8),
                axis=AX, op=ALU.add))
            p.op("vector", lambda E: E.tensor_reduce(
                out=r256[0:1, 256:512],
                in_=rowsB[0:1, :].rearrange("o (h c) -> o c h", h=8),
                axis=AX, op=ALU.add))
            p.op("vector", lambda E: E.tensor_scalar_mul(
                out=r256[0:1, 0:256], in0=r256[0:1, 0:256], scalar1=1.0 / 131072.0))
            p.op("vector", lambda E: E.tensor_scalar_mul(
                out=r256[0:1, 256:512], in0=r256[0:1, 256:512],
                scalar1=1.0 / 131072.0))
            p.op("vector", lambda E: E.tensor_tensor(
                out=rowsC[0:1, 0:256], in0=r256[0:1, 0:256], in1=r256[0:1, 0:256],
                op=ALU.mult))
            tvv = p.op("vector", lambda E: E.tensor_tensor(
                out=r256[0:1, 256:512], in0=r256[0:1, 256:512],
                in1=rowsC[0:1, 0:256], op=ALU.subtract))
            sq = p.op("scalar", lambda E: E.activation(
                out=r256[0:1, 256:512], in_=r256[0:1, 256:512], func=AF.Sqrt,
                bias=c.cst[0:1, 0:1]), deps=(tvv,))
            p.op("vector", lambda E: E.reciprocal(
                out=r256[0:1, 256:512], in_=r256[0:1, 256:512]), deps=(sq,))
            p.barrier()
            prevev = None
            for i, off in enumerate((0, 128, 256, 384)):
                tt = p.op("tensor", lambda E, off=off: E.matmul(
                    psT[:, 0:1], r256[0:1, off:off + 128],
                    c.ones_row_f[0:1, 0:1], start=True, stop=True),
                    deps=(prevev,), inc=True)
                prevev = p.op("vector", lambda E, i=i: E.tensor_copy(
                    out=colt[:, i:i + 1], in_=psT[:, 0:1]), deps=(tt,))
            p.barrier()
            gcol = sb3("l0_gcol", [128, 2], F32)
            bqc = sb3("l0_bqc", [128, 16], F32)
            l1 = p.dma("sync", gcol[:, :], W["l0_gam_col"][:, :])
            l3 = p.dma("sync", bqc[:, :], W["l0_bq_col"][:, :])
            p.op("vector", lambda E: E.tensor_tensor(
                out=B1[:, 0:2], in0=colt[:, 2:4], in1=gcol[:, 0:2], op=ALU.mult),
                deps=(l1,))
            for par in range(2):
                p.op("vector", lambda E, par=par: E.tensor_tensor(
                    out=A1[:, par:16:2],
                    in0=bqc[:, par:16:2],
                    in1=colt[:, par:par + 1].broadcast_to([128, 8]),
                    op=ALU.subtract), deps=(l3,))
            p.barrier()


def emit_layer0(p, c, W, stack):
    nc = p.nc
    sb = lambda name, shape, dt: stack.enter_context(nc.sbuf_tensor(name, shape, dt))
    A1 = sb("l0_A1", [128, 16], F32)
    B1 = sb("l0_B1", [128, 4], F32)
    betc = sb("l0_betc", [128, 2], F32)
    p.dma("sync", betc[:, :], W["l0_bet_col"][:, :])
    emit_l0_stats(p, c, W, stack, A1, B1)

    s_dram = nc.dram_tensor("l0_s_scratch", [NT, 128, D], BF16)
    v_dram = nc.dram_tensor("l0_v_scratch", [NT, 128, D], BF16)
    zbuf = sb("l0_z", [128, NT], F32)
    bcolA = sb("l0_bcolA", [128, 16], F32)

    with contextlib.ExitStack() as st:
        sb2 = lambda n, s, d: st.enter_context(nc.sbuf_tensor(n, s, d))
        qch = sb2("l0_qch", [128, NC, T], BF16)
        kch = sb2("l0_kch", [128, NC, T], BF16)
        with contextlib.ExitStack() as st2:
            xn_ch = st2.enter_context(nc.sbuf_tensor("l0_xn_ch", [128, NC, T], BF16))
            with contextlib.ExitStack() as st2b:
                c.xn_t = st2b.enter_context(nc.sbuf_tensor("l0_xn_t", [128, NT, D], BF16))
                pt1 = st2b.enter_context(nc.psum_tensor("l0_pt1", [128, 128], BF16))
                pt2 = st2b.enter_context(nc.psum_tensor("l0_pt2", [128, 128], BF16))
                emit_ln(p, c)
                p.barrier()
                emit_t2ch(p, c, c.xn_t, xn_ch, (pt1, pt2), col0=0)
                p.barrier()
            with contextlib.ExitStack() as st3:
                wsl = [st3.enter_context(nc.sbuf_tensor(f"l0_wsl{i}", [128, NC, 128], BF16))
                       for i in range(2)]
                wv_sb = st3.enter_context(nc.sbuf_tensor("l0_wv", [128, NC, D], BF16))
                vout = [st3.enter_context(nc.sbuf_tensor(f"l0_vout{i}", [128, D], BF16))
                        for i in range(2)]
                ps2 = [st3.enter_context(nc.psum_tensor(f"l0_ps2{i}", [128, 512], F32))
                       for i in range(2)]
                lv = p.dma("gpsimd", wv_sb[:, :, :],
                           W["l0_wvT"].rearrange("(c p) n -> p c n", p=128))
                rot = Rot(ps2)
                vrot = Rot(vout)
                for ti in range(NT):
                    vb, vbusy, vj = vrot.get()
                    evs = []
                    for nh in range(2):
                        bank, busy, j = rot.get()
                        mm = mm_group(p, bank[:, :],
                                      [(xn_ch[:, kc, ti * 128:(ti + 1) * 128],
                                        wv_sb[:, kc, nh * 512:(nh + 1) * 512])
                                       for kc in range(NC)], deps=(lv, busy, vbusy))
                        ev = p.op("vector", lambda E, nh=nh, bank=bank, vb=vb:
                                  E.tensor_copy(out=vb[:, nh * 512:(nh + 1) * 512],
                                                in_=bank[:, :]), deps=(mm,))
                        rot.set_busy(j, ev)
                        evs.append(ev)
                    vd = p.dma("scalar", v_dram[ti], vb[:, :], deps=tuple(evs))
                    vrot.set_busy(vj, vd)
                wrot = Rot(wsl)
                for dst, wname in ((qch, "l0_wqT"), (kch, "l0_wkT")):
                    wsrc = W[wname].rearrange("(c p) n -> p c n", p=128)
                    slots = {}
                    for oc in range(2):
                        buf, busy, j = wrot.get()
                        slots[oc] = (buf, p.dma("gpsimd", buf[:, :, :],
                                                wsrc[:, :, oc * 128:(oc + 1) * 128],
                                                deps=(busy,)), j)
                    for oc in range(NC):
                        buf, lt, slot = slots.pop(oc)
                        lastev = None
                        for w4 in range(4):
                            bank, busy, j = rot.get()
                            mm = mm_group(p, bank[:, :],
                                          [(buf[:, kc, :],
                                            xn_ch[:, kc, w4 * 512:(w4 + 1) * 512])
                                           for kc in range(NC)], deps=(lt, busy))
                            lastev = p.op("vector",
                                          lambda E, dst=dst, oc=oc, w4=w4, bank=bank:
                                          E.tensor_copy(
                                              out=dst[:, oc, w4 * 512:(w4 + 1) * 512],
                                              in_=bank[:, :]), deps=(mm,))
                            rot.set_busy(j, lastev)
                        wrot.set_busy(slot, lastev)
                        if oc + 2 < NC:
                            buf2, busy2, j2 = wrot.get()
                            slots[oc + 2] = (buf2, p.dma(
                                "gpsimd", buf2[:, :, :],
                                wsrc[:, :, (oc + 2) * 128:(oc + 3) * 128],
                                deps=(busy2,)), j2)
                p.barrier()
        with contextlib.ExitStack() as st2:
            sb3 = lambda n, s, d: st2.enter_context(nc.sbuf_tensor(n, s, d))
            rowbf = sb3("l0_rowbf", [1, 512], BF16)
            rowb = sb3("l0_rowb", [1, 256], BF16)
            vsl = [sb3(f"l0_vsl{i}", [128, D], BF16) for i in range(4)]
            sout = [sb3(f"l0_sout{i}", [128, D], BF16) for i in range(2)]
            pj = [sb3(f"l0_pj{i}", [128, 256], BF16) for i in range(2)]
            psA = [st2.enter_context(nc.psum_tensor(f"l0_psA{i}", [128, 512], F32))
                   for i in range(2)]
            psO = [st2.enter_context(nc.psum_tensor(f"l0_psO{i}", [128, 512], F32))
                   for i in range(4)]
            psR = st2.enter_context(nc.psum_tensor("l0_psR", [1, 512], F32))
            psTb = st2.enter_context(nc.psum_tensor("l0_psTbb", [128, 128], F32))
            prev = None
            maxq_tok = None
            for which, srcb in ((0, qch), (1, kch)):
                for w4 in range(4):
                    for kc in range(NC):
                        sq = p.op("scalar", lambda E, srcb=srcb, kc=kc, w4=w4:
                                  E.activation(out=c.sc_f[:, 0:512],
                                               in_=srcb[:, kc, w4 * 512:(w4 + 1) * 512],
                                               func=AF.Square), deps=(prev,))
                        prev = p.op("tensor", lambda E, s=(kc == 0), e=(kc == 7):
                                    E.matmul(psR[0:1, :], c.ones_col_f[:, :],
                                             c.sc_f[:, 0:512], start=s, stop=e,
                                             skip_group_check=True),
                                    deps=(sq,), inc=True)
                    if which == 0:
                        prev = p.op("vector", lambda E, w4=w4: E.tensor_reduce(
                            out=c.stats[0:1, 0, w4:w4 + 1], in_=psR[0:1, :],
                            axis=AX, op=ALU.max), deps=(prev,))
                    else:
                        r1 = p.op("vector", lambda E: E.tensor_scalar_mul(
                            out=c.sc_f2[0:1, 0:512], in0=psR[0:1, :],
                            scalar1=c.stats[0:1, 0, 4:5]), deps=(prev, maxq_tok))
                        r2 = p.op("scalar", lambda E: E.activation(
                            out=c.sc_f2[0:1, 0:512], in_=c.sc_f2[0:1, 0:512],
                            func=AF.Sqrt), deps=(r1,))
                        rb = p.op("vector", lambda E: E.tensor_scalar_mul(
                            out=rowbf[0:1, :], in0=c.sc_f2[0:1, 0:512],
                            scalar1=-1.0), deps=(r2,))
                        prevt = rb
                        for tl in range(4):
                            ttr = p.op("tensor", lambda E, tl=tl: E.matmul(
                                psTb[:, 0:1], rowbf[0:1, tl * 128:(tl + 1) * 128],
                                c.ones_row_bf[0:1, 0:1], start=True, stop=True),
                                deps=(prevt,), inc=True)
                            prevt = p.op("vector",
                                         lambda E, w4=w4, tl=tl: E.tensor_scalar_mul(
                                             out=bcolA[:, w4 * 4 + tl:w4 * 4 + tl + 1],
                                             in0=psTb[:, 0:1], scalar1=1.0 / 32.0),
                                         deps=(ttr,))
                        prev = prevt
                if which == 0:
                    maxq_tok = p.op("vector", lambda E: E.tensor_reduce(
                        out=c.stats[0:1, 0, 4:5], in_=c.stats[0:1, 0, 0:4],
                        axis=AX, op=ALU.max), deps=(prev,))
                    prev = maxq_tok
            p.barrier()
            rot = Rot(psA)
            for ti in range(NT):
                zacc = []
                for qw in range(4):
                    bank, busy, j = rot.get()
                    mm = mm_group(p, bank[:, :],
                                  [(kch[:, kc, ti * 128:(ti + 1) * 128],
                                    qch[:, kc, qw * 512:(qw + 1) * 512])
                                   for kc in range(NC)], deps=(busy,))
                    ex = p.op("scalar", lambda E, ti=ti, qw=qw, bank=bank:
                              E.activation(out=c.sc_f[:, 0:512], in_=bank[:, :],
                                           func=AF.Exp, scale=1.0 / 32.0,
                                           bias=bcolA[:, ti:ti + 1],
                                           accum_out=c.stats[:, ti, qw:qw + 1]),
                              deps=(mm,))
                    rot.set_busy(j, ex)
                    zacc.append(ex)
                zs = p.op("vector", lambda E, ti=ti: E.tensor_reduce(
                    out=c.stats[:, ti, 4:5], in_=c.stats[:, ti, 0:4], axis=AX,
                    op=ALU.add), deps=tuple(zacc))
                p.op("vector", lambda E, ti=ti: E.reciprocal(
                    out=zbuf[:, ti:ti + 1], in_=c.stats[:, ti, 4:5]), deps=(zs,))
            p.barrier()
            prevrow = None
            for iw in range(8):
                nqt = 2 * iw + 2
                for ic in range(2):
                    ti = iw * 2 + ic
                    ttr = p.op("tensor", lambda E, ti=ti: E.transpose(
                        psR[0:1, 0:128],
                        bcolA[:, ti:ti + 1], c.ident_f[:, :]),
                        deps=(prevrow,), inc=True)
                    prevrow = p.op("vector", lambda E, ic=ic: E.tensor_scalar_mul(
                        out=rowb[0:1, ic * 128:(ic + 1) * 128], in0=psR[0:1, 0:128],
                        scalar1=32.0), deps=(ttr,))
                prot = Rot(list(zip(pj, psA)))
                vrot2 = Rot(vsl) if iw == 0 else vrot2
                srot = Rot(sout) if iw == 0 else srot
                tvs = []
                for qt in range(nqt):
                    (pbuf, bank), busy, j = prot.get()
                    vb, vbusy, vj = vrot2.get()
                    vld = p.dma("scalar", vb[:, :], v_dram[qt], deps=(vbusy,))
                    pairs = [(qch[:, kc, qt * 128:(qt + 1) * 128],
                              kch[:, kc, iw * 256:(iw + 1) * 256])
                             for kc in range(NC)]
                    pairs.append((c.ones_row_bf[:, :], rowb[0:1, 0:256]))
                    mm = mm_group(p, bank[:, 0:256], pairs, deps=(busy, prevrow))
                    dk = qt - 2 * iw
                    if dk >= 0:
                        mm = p.op("vector", lambda E, bank=bank, dk=dk:
                                  E.tensor_tensor(out=bank[:, 0:256],
                                                  in0=bank[:, 0:256],
                                                  in1=c.mask4[:, dk, 0:256],
                                                  op=ALU.add), deps=(mm,))
                    ex = p.op("scalar", lambda E, pbuf=pbuf, bank=bank:
                              E.activation(out=pbuf[:, :], in_=bank[:, 0:256],
                                           func=AF.Exp, scale=1.0 / 32.0), deps=(mm,))
                    tvlast = None
                    for ic in range(2):
                        for vh in range(2):
                            tvlast = p.op(
                                "tensor",
                                lambda E, pbuf=pbuf, vb=vb, ic=ic, vh=vh,
                                s=(qt == 0), e=(qt == nqt - 1):
                                E.matmul(psO[ic * 2 + vh][:, :],
                                         pbuf[:, ic * 128:(ic + 1) * 128],
                                         vb[:, vh * 512:(vh + 1) * 512],
                                         start=s, stop=e, skip_group_check=True),
                                deps=(ex, vld) if (ic == 0 and vh == 0) else (),
                                inc=(ic == 1 and vh == 1))
                    prot.set_busy(j, tvlast)
                    vrot2.set_busy(vj, tvlast)
                    tvs.append(tvlast)
                for ic in range(2):
                    ti = iw * 2 + ic
                    sb_, sbusy, sj = srot.get()
                    evs = []
                    for vh in range(2):
                        evs.append(p.op("vector", lambda E, ti=ti, ic=ic, vh=vh, sb_=sb_:
                                        E.tensor_scalar_mul(
                                            out=sb_[:, vh * 512:(vh + 1) * 512],
                                            in0=psO[ic * 2 + vh][:, :],
                                            scalar1=zbuf[:, ti:ti + 1]),
                                        deps=(tvs[-1], sbusy)))
                    sd = p.dma("sync", s_dram[ti], sb_[:, :], deps=tuple(evs))
                    srot.set_busy(sj, sd)
                p.barrier()

    zch_dram = nc.dram_tensor("l0_zch_scratch", [NC, 128, T], BF16)
    with contextlib.ExitStack() as st:
        sb2 = lambda n, s, d: st.enter_context(nc.sbuf_tensor(n, s, d))
        xn_ch = sb2("l0_xn_ch2", [128, NC, T], BF16)
        with contextlib.ExitStack() as st2:
            c.xn_t = st2.enter_context(nc.sbuf_tensor("l0_xn_t2", [128, NT, D], BF16))
            pt1 = st2.enter_context(nc.psum_tensor("l0_pt1b", [128, 128], BF16))
            pt2 = st2.enter_context(nc.psum_tensor("l0_pt2b", [128, 128], BF16))
            emit_ln(p, c)
            p.barrier()
            emit_t2ch(p, c, c.xn_t, xn_ch, (pt1, pt2), col0=0)
            p.barrier()
        kT = sb2("l0_kT", [128, 64], BF16)
        dn_sb = sb2("l0_dn", [128, NC, D], BF16)
        ldn = p.dma("gpsimd", dn_sb[:, :, :],
                    W["l0_downT"].rearrange("(c p) n -> p c n", p=128))
        with contextlib.ExitStack() as st2:
            sub = st2.enter_context(nc.sbuf_tensor("l0_sub", [64, 128], F32))
            subn = st2.enter_context(nc.sbuf_tensor("l0_subn", [64, 128], BF16))
            psK = st2.enter_context(nc.psum_tensor("l0_psK", [128, 128], BF16))
            lk0 = p.dma("sync", sub[0:32, :], W["l0_sub0"][:, :])
            lk1 = p.dma("sync", sub[32:64, :], W["l0_sub1"][:, :])
            sq = p.op("scalar", lambda E: E.activation(
                out=c.sc_f[0:64, 0:128], in_=sub[:, :], func=AF.Square),
                deps=(lk0, lk1))
            rd = p.op("vector", lambda E: E.tensor_reduce(
                out=c.sc_f2[0:64, 0:1], in_=c.sc_f[0:64, 0:128], axis=AX,
                op=ALU.add), deps=(sq,))
            s2 = p.op("scalar", lambda E: E.activation(
                out=c.sc_f2[0:64, 1:2], in_=c.sc_f2[0:64, 0:1], func=AF.Sqrt),
                deps=(rd,))
            mx = p.op("vector", lambda E: E.tensor_scalar_max(
                out=c.sc_f2[0:64, 1:2], in0=c.sc_f2[0:64, 1:2], scalar1=1e-12),
                deps=(s2,))
            rc = p.op("vector", lambda E: E.reciprocal(
                out=c.sc_f2[0:64, 2:3], in_=c.sc_f2[0:64, 1:2]), deps=(mx,))
            nm = p.op("vector", lambda E: E.tensor_scalar_mul(
                out=subn[:, :], in0=sub[:, :], scalar1=c.sc_f2[0:64, 2:3]),
                deps=(rc,))
            tt = p.op("tensor", lambda E: E.transpose(
                psK[:, 0:64], subn[:, :], c.ident_bf[0:64, 0:64]),
                deps=(nm,), inc=True)
            p.op("vector", lambda E: E.tensor_copy(out=kT[:, :], in_=psK[:, 0:64]),
                 deps=(tt,))
            p.barrier()
        qnw = sb2("l0_qnw", [128, 16, 512], BF16)
        scr = sb2("l0_scr", [128, 512], F32)
        ubuf = sb2("l0_u", [128, 8, 32], BF16)
        vbuf = sb2("l0_v", [128, 8, 32], BF16)
        aggt = sb2("l0_agg", [128, 1024], F32)
        eht = sb2("l0_eh", [128, 1024], BF16)
        zt = sb2("l0_zt", [128, 8, 128], BF16)
        wqs = [sb2(f"l0_wqs{i}", [128, NC, 128], BF16) for i in range(2)]
        rnw = sb2("l0_rnw", [1, 512], F32)
        rcolw = sb2("l0_rcolw", [128, 16, 4], F32)
        psQ = [st.enter_context(nc.psum_tensor(f"l0_psQ{i}", [128, 512], F32))
               for i in range(2)]
        psRf = st.enter_context(nc.psum_tensor("l0_psRb", [1, 512], F32))
        psS = st.enter_context(nc.psum_tensor("l0_psSc", [128, 512], F32))
        psT = st.enter_context(nc.psum_tensor("l0_psTb", [128, 128], BF16))
        psT2 = st.enter_context(nc.psum_tensor("l0_psT2b", [128, 128], F32))
        wqsrc = W["l0_fwq"].rearrange("(c p) n -> p c n", p=128)
        zchd = zch_dram
        prev_zdma = None
        for w4 in range(4):
            wrot = Rot(wqs)
            slots = {}
            for oc in range(2):
                buf, busy, j = wrot.get()
                slots[oc] = (buf, p.dma("gpsimd", buf[:, :, :],
                                        wqsrc[:, :, oc * 128:(oc + 1) * 128],
                                        deps=(busy,)), j)
            qrot = Rot(psQ)
            prev_mmn = None
            prev_r2 = None
            prevev = None
            for oc in range(16):
                buf, lt, slot = slots.pop(oc)
                bank, busy, j = qrot.get()
                mm = mm_group(p, bank[:, :],
                              [(buf[:, kc, :], xn_ch[:, kc, w4 * 512:(w4 + 1) * 512])
                               for kc in range(NC)], deps=(lt, busy))
                e1 = p.op("vector", lambda E, oc=oc, bank=bank: E.tensor_scalar(
                    out=c.sc_f[:, 0:512], in0=bank[:, :], scalar1=A1[:, oc:oc + 1],
                    scalar2=B1[:, (oc % 2):(oc % 2) + 1], op0=ALU.add, op1=ALU.mult),
                    deps=(mm,))
                e2 = p.op("vector", lambda E, oc=oc: E.tensor_scalar(
                    out=qnw[:, oc, :], in0=c.sc_f[:, 0:512],
                    scalar1=betc[:, (oc % 2):(oc % 2) + 1], scalar2=None,
                    op0=ALU.add), deps=(e1,))
                qrot.set_busy(j, e2)
                sq = p.op("scalar", lambda E, oc=oc: E.activation(
                    out=c.sc_f[:, 0:512], in_=qnw[:, oc, :], func=AF.Square),
                    deps=(e2, prev_mmn, prevev))
                mmn = p.op("tensor", lambda E: E.matmul(
                    psRf[0:1, :], c.ones_col_f[:, :], c.sc_f[:, 0:512],
                    start=True, stop=True, skip_group_check=True),
                    deps=(sq, prev_r2), inc=True)
                prev_mmn = mmn
                r2 = p.op("scalar", lambda E: E.activation(
                    out=rnw[0:1, :], in_=psRf[0:1, :], func=AF.Sqrt), deps=(mmn,))
                prev_r2 = r2
                r3 = p.op("vector", lambda E: E.tensor_scalar_max(
                    out=rnw[0:1, :], in0=rnw[0:1, :], scalar1=1e-12), deps=(r2,))
                r4 = p.op("vector", lambda E: E.reciprocal(
                    out=rnw[0:1, :], in_=rnw[0:1, :]))
                r4b = p.op("vector", lambda E: E.tensor_copy(
                    out=c.sc_f[0:1, 0:512], in_=rnw[0:1, :]))
                prevev = r4b
                for tl in range(4):
                    ttr = p.op("tensor", lambda E, tl=tl: E.matmul(
                        psT2[:, 0:1], c.sc_f[0:1, tl * 128:(tl + 1) * 128],
                        c.ones_row_f[0:1, 0:1], start=True, stop=True),
                        deps=(prevev,), inc=True)
                    prevev = p.op("vector", lambda E, oc=oc, tl=tl: E.tensor_copy(
                        out=rcolw[:, oc, tl:tl + 1], in_=psT2[:, 0:1]), deps=(ttr,))
                if oc + 2 < 16:
                    buf2, busy2, j2 = wrot.get()
                    slots[oc + 2] = (buf2, p.dma("gpsimd", buf2[:, :, :],
                                                 wqsrc[:, :, (oc + 2) * 128:(oc + 3) * 128],
                                                 deps=(busy2, mm)), j2)
            p.barrier()
            for tl in range(4):
                ti = w4 * 4 + tl
                prevex = None
                for oc in range(16):
                    side = oc % 2
                    mm = p.op("tensor", lambda E, oc=oc, tl=tl, side=side:
                              E.matmul(psS[:, 0:32],
                                       qnw[:, oc, tl * 128:(tl + 1) * 128],
                                       kT[:, side * 32:side * 32 + 32],
                                       start=True, stop=True,
                                       skip_group_check=True),
                              deps=(prevex,), inc=True)
                    prevex = p.op("vector", lambda E, oc=oc, tl=tl: E.tensor_scalar_mul(
                        out=scr[:, oc * 32:(oc + 1) * 32], in0=psS[:, 0:32],
                        scalar1=rcolw[:, oc, tl:tl + 1]), deps=(mm,))
                st8 = c.stats[:, ti, :]
                g32 = scr[:, :].rearrange("p (g n) -> p g n", g=16)
                m1 = p.op("vector", lambda E: E.tensor_reduce(
                    out=c.sc_f[:, 0:16], in_=g32, axis=AX, op=ALU.max),
                    deps=(prevex,))
                cur = m1
                for it in range(3):
                    eq = p.op("vector", lambda E, it=it: E.tensor_tensor(
                        out=c.sc_f2[:, 0:512],
                        in0=scr[:, :],
                        in1=c.sc_f[:, 16 * it:16 * (it + 1)].rearrange(
                            "p (g o) -> p g o", o=1).broadcast_to([128, 16, 32]),
                        op=ALU.is_ge), deps=(cur,))
                    sup = p.op("vector", lambda E: E.scalar_tensor_tensor(
                        out=scr[:, :], in0=c.sc_f2[:, 0:512], scalar=-1e30,
                        in1=scr[:, :], op0=ALU.mult, op1=ALU.add), deps=(eq,))
                    cur = p.op("vector", lambda E, it=it: E.tensor_reduce(
                        out=c.sc_f[:, 16 * (it + 1):16 * (it + 2)],
                        in_=g32, axis=AX, op=ALU.max), deps=(sup,))
                prevex2 = None
                for oc in range(16):
                    side = oc % 2
                    mm = p.op("tensor", lambda E, oc=oc, tl=tl, side=side:
                              E.matmul(psS[:, 0:32],
                                       qnw[:, oc, tl * 128:(tl + 1) * 128],
                                       kT[:, side * 32:side * 32 + 32],
                                       start=True, stop=True,
                                       skip_group_check=True),
                              deps=(prevex2, cur), inc=True)
                    prevex2 = p.op("vector", lambda E, oc=oc, tl=tl: E.tensor_scalar_mul(
                        out=scr[:, oc * 32:(oc + 1) * 32], in0=psS[:, 0:32],
                        scalar1=rcolw[:, oc, tl:tl + 1]), deps=(mm,))
                keep = p.op("vector", lambda E: E.tensor_tensor(
                    out=c.sc_f2[:, 0:512], in0=scr[:, :],
                    in1=c.sc_f[:, 48:64].rearrange("p (g o) -> p g o", o=1).broadcast_to([128, 16, 32]),
                    op=ALU.is_ge), deps=(prevex2,))
                sb1 = p.op("vector", lambda E: E.tensor_tensor(
                    out=scr[:, :], in0=scr[:, :],
                    in1=c.sc_f[:, 0:16].rearrange("p (g o) -> p g o", o=1).broadcast_to([128, 16, 32]),
                    op=ALU.subtract), deps=(keep,))
                exx = p.op("scalar", lambda E: E.activation(
                    out=scr[:, :], in_=scr[:, :], func=AF.Exp), deps=(sb1,))
                msk = p.op("vector", lambda E: E.tensor_tensor(
                    out=scr[:, :], in0=scr[:, :], in1=c.sc_f2[:, 0:512],
                    op=ALU.mult), deps=(exx,))
                zz = p.op("vector", lambda E: E.tensor_reduce(
                    out=c.sc_f[:, 16:32], in_=g32, axis=AX, op=ALU.add),
                    deps=(msk,))
                rz = p.op("vector", lambda E: E.reciprocal(
                    out=c.sc_f[:, 16:32], in_=c.sc_f[:, 16:32]), deps=(zz,))
                nr = p.op("vector", lambda E: E.tensor_tensor(
                    out=scr[:, :], in0=scr[:, :],
                    in1=c.sc_f[:, 16:32].rearrange("p (g o) -> p g o", o=1).broadcast_to([128, 16, 32]),
                    op=ALU.mult), deps=(rz,))
                cu = p.op("vector", lambda E: E.tensor_copy(
                    out=ubuf[:, :, :],
                    in_=scr[:, :].rearrange("p (h s n) -> p h s n", h=8, s=2)[:, :, 0, :]),
                    deps=(nr,))
                cv = p.op("vector", lambda E: E.tensor_copy(
                    out=vbuf[:, :, :],
                    in_=scr[:, :].rearrange("p (h s n) -> p h s n", h=8, s=2)[:, :, 1, :]),
                    deps=(nr,))
                prevagg = None
                for h in range(8):
                    u_ap = ubuf[:, h, :].rearrange("p (n o) -> p n o", o=1).broadcast_to([128, 32, 32])
                    v_ap = vbuf[:, h, :].rearrange("p (o n) -> p o n", o=1).broadcast_to([128, 32, 32])
                    if h == 0:
                        prevagg = p.op("vector", lambda E, u_ap=u_ap, v_ap=v_ap:
                                       E.tensor_tensor(out=aggt[:, :], in0=u_ap,
                                                       in1=v_ap, op=ALU.mult),
                                       deps=(cu, cv))
                    else:
                        tmp = p.op("vector", lambda E, u_ap=u_ap, v_ap=v_ap:
                                   E.tensor_tensor(out=c.sc_f[:, 0:1024], in0=u_ap,
                                                   in1=v_ap, op=ALU.mult),
                                   deps=(prevagg,))
                        prevagg = p.op("vector", lambda E: E.tensor_tensor(
                            out=aggt[:, :], in0=aggt[:, :], in1=c.sc_f[:, 0:1024],
                            op=ALU.add), deps=(tmp,))
                gels = []
                for eh2 in range(2):
                    qb, qbusy, qj = qrot.get()
                    mm = mm_group(p, qb[:, :],
                                  [(xn_ch[:, kc, ti * 128:(ti + 1) * 128],
                                    dn_sb[:, kc, eh2 * 512:(eh2 + 1) * 512])
                                   for kc in range(NC)], deps=(ldn, prevagg, qbusy))
                    g_ = p.op("scalar", lambda E, eh2=eh2, qb=qb: E.activation(
                        out=eht[:, eh2 * 512:(eh2 + 1) * 512], in_=qb[:, :],
                        func=AF.Gelu), deps=(mm,))
                    gels.append(g_)
                    qrot.set_busy(qj, g_)
                zm = p.op("vector", lambda E: E.tensor_tensor(
                    out=c.sc_f[:, 0:1024], in0=aggt[:, :], in1=eht[:, :],
                    op=ALU.mult), deps=tuple(gels))
                zc = p.op("vector", lambda E: E.tensor_copy(
                    out=eht[:, :], in_=c.sc_f[:, 0:1024]), deps=(zm,))
                prevtt = zc
                for kc in range(NC):
                    ttk = p.op("tensor", lambda E, kc=kc: E.transpose(
                        psT[:, :], eht[:, kc * 128:(kc + 1) * 128], c.ident_bf[:, :]),
                        deps=(prevtt,), inc=True)
                    prevtt = p.op("vector", lambda E, kc=kc: E.tensor_copy(
                        out=zt[:, kc, :], in_=psT[:, :]), deps=(ttk, prev_zdma))
                prev_zdma = p.dma("sync", zchd[:, :, ti * 128:(ti + 1) * 128].rearrange(
                    "c p n -> p c n"), zt[:, :, :], deps=(prevtt,))
            p.barrier()
    with contextlib.ExitStack() as st:
        sb2 = lambda n, s, d: st.enter_context(nc.sbuf_tensor(n, s, d))
        up_sb = sb2("l0_up_sb", [128, NC, D], BF16)
        zsl = [sb2(f"l0_zsl{i}", [128, NC, 128], BF16) for i in range(2)]
        ps2 = [st.enter_context(nc.psum_tensor(f"l0_psF{i}", [128, 512], F32))
               for i in range(2)]
        lu = p.dma("gpsimd", up_sb[:, :, :],
                   W["l0_up"].rearrange("(c p) n -> p c n", p=128))
        stl = [sb2(f"l0_stl{i}", [128, D], BF16) for i in range(2)]
        zrd = zch_dram.rearrange("c p n -> p c n")
        zrot = Rot(zsl)
        srot = Rot(stl)
        slots = {}
        for ti in range(2):
            buf, busy, j = zrot.get()
            slots[ti] = (buf, p.dma("gpsimd", buf[:, :, :],
                                    zrd[:, :, ti * 128:(ti + 1) * 128],
                                    deps=(busy,)), j)
        rot = Rot(ps2)
        prev_add = None
        for ti in range(NT):
            buf, lt, slot = slots.pop(ti)
            sb_, sbusy, sj = srot.get()
            sld = p.dma("sync", sb_[:, :], s_dram[ti], deps=(sbusy,))
            lastev = None
            for nh in range(2):
                bank, busy, j = rot.get()
                mm = mm_group(p, bank[:, :],
                              [(buf[:, kc, :], up_sb[:, kc, nh * 512:(nh + 1) * 512])
                               for kc in range(NC)], deps=(lu, lt, busy))
                g1 = p.op("scalar", lambda E, bank=bank: E.activation(
                    out=c.sc_f[:, 0:512], in_=bank[:, :], func=AF.Silu),
                    deps=(mm, prev_add))
                g2 = p.op("vector", lambda E, nh=nh, sb_=sb_: E.tensor_tensor(
                    out=c.sc_f2[:, 0:512], in0=c.sc_f[:, 0:512],
                    in1=sb_[:, nh * 512:(nh + 1) * 512], op=ALU.mult),
                    deps=(g1, sld))
                prev_add = p.op("vector", lambda E, ti=ti, nh=nh: E.tensor_tensor(
                    out=c.x[:, ti, nh * 512:(nh + 1) * 512],
                    in0=c.x[:, ti, nh * 512:(nh + 1) * 512],
                    in1=c.sc_f2[:, 0:512], op=ALU.add), deps=(g2,))
                lastev = prev_add
                rot.set_busy(j, prev_add)
                srot.set_busy(sj, prev_add)
            if ti + 2 < NT:
                buf2, busy2, j2 = zrot.get()
                slots[ti + 2] = (buf2, p.dma("gpsimd", buf2[:, :, :],
                                             zrd[:, :, (ti + 2) * 128:(ti + 3) * 128],
                                             deps=(busy2, lastev)), j2)
        p.barrier()



def emit_gj(p, eng, buf, aux, tmp, nrows, rowlen, off=0, dep0=None):
    prev = dep0
    rg = 16
    bview = buf[:, off:off + nrows * rowlen]
    b3 = bview.rearrange("p (r l) -> p r l", l=rowlen)
    for j in range(nrows):
        o = off + j * rowlen
        rp = p.op("vector", lambda E, o=o, j=j: E.reciprocal(
            out=aux[:, 0:1], in_=buf[:, o + j:o + j + 1]), deps=(prev,))
        nr = p.op(eng, lambda E, o=o: E.tensor_scalar_mul(
            out=buf[:, o:o + rowlen], in0=buf[:, o:o + rowlen],
            scalar1=aux[:, 0:1]), deps=(rp,))
        zp = p.op(eng, lambda E, o=o, j=j: E.tensor_scalar_mul(
            out=buf[:, o + j:o + j + 1], in0=buf[:, o + j:o + j + 1],
            scalar1=0.0), deps=(nr,))
        nf = p.op(eng, lambda E, j=j: E.tensor_scalar_mul(
            out=aux[:, 2:2 + nrows], in0=b3[:, :, j], scalar1=-1.0), deps=(zp,))
        for r0 in range(0, nrows, rg):
            gl = min(rg, nrows - r0)
            f_ap = aux[:, 2 + r0:2 + r0 + gl].rearrange(
                "p (r o) -> p r o", o=1).broadcast_to([128, gl, rowlen])
            rj_ap = buf[:, o:o + rowlen].rearrange(
                "p (o l) -> p o l", o=1).broadcast_to([128, gl, rowlen])
            tm = p.op(eng, lambda E, f_ap=f_ap, rj_ap=rj_ap, gl=gl: E.tensor_tensor(
                out=tmp[:, 0:gl * rowlen], in0=f_ap, in1=rj_ap, op=ALU.mult),
                deps=(nf,))
            prev = p.op(eng, lambda E, r0=r0, gl=gl, off=off: E.tensor_tensor(
                out=buf[:, off + r0 * rowlen:off + (r0 + gl) * rowlen],
                in0=buf[:, off + r0 * rowlen:off + (r0 + gl) * rowlen],
                in1=tmp[:, 0:gl * rowlen], op=ALU.add), deps=(tm,))
    return prev


def emit_layer1(p, c, W, stack):
    nc = p.nc
    sb = lambda name, shape, dt: stack.enter_context(nc.sbuf_tensor(name, shape, dt))
    NH, HD, NPROB = 16, 64, 256
    s_dram = nc.dram_tensor("l1_s_scr", [NT, 128, D], BF16)
    q_dram = nc.dram_tensor("l1_q_scr", [NC, 128, T], BF16)
    k_dram = nc.dram_tensor("l1_k_scr", [NT, 128, D], BF16)
    v_dram = nc.dram_tensor("l1_v_scr", [NT, 128, D], BF16)
    gb_dram = nc.dram_tensor("l1_gb_scr", [NPROB, 64, 128], F32)
    w_dram = nc.dram_tensor("l1_w_scr", [NPROB, 64, 64], F32)
    cr_dram = nc.dram_tensor("l1_cr_scr", [NPROB, 8, 72], F32)
    x_dram = nc.dram_tensor("l1_x_scr", [NPROB, 8, 64], F32)
    eye4 = sb("l1_eye4", [64, 64], F32)
    g0 = p.op("gpsimd", lambda E: E.memset(eye4[:, :], 0.0))
    g1 = p.op("gpsimd", lambda E: E.affine_select(
        out=eye4[:, :], in_=eye4[:, :], compare_op=ALU.not_equal, fill=1e-4,
        base=0, pattern=[[-1, 64]], channel_multiplier=1), deps=(g0,))

    with contextlib.ExitStack() as stA:
        sb2 = lambda n, s, d: stA.enter_context(nc.sbuf_tensor(n, s, d))
        c.xn_t = sb2("l1_xn_t", [128, NT, D], BF16)
        emit_ln(p, c, extra_rstd2=True)
        p.barrier()
        with contextlib.ExitStack() as st2:
            sb3 = lambda n, s, d: st2.enter_context(nc.sbuf_tensor(n, s, d))
            ln2ch = sb3("l1_ln2ch", [128, NC, T], BF16)
            ln2t = sb3("l1_ln2t", [128, 1024], BF16)
            gbc = sb3("l1_gbc", [128, D], F32)
            bbc = sb3("l1_bbc", [128, D], F32)
            grow = sb3("l1_grow", [1, D], F32)
            wsl = [sb3(f"l1_wsl{i}", [128, NC, 128], BF16) for i in range(2)]
            wkv = sb3("l1_wkv", [128, NC, 1024], BF16)
            otile = [sb3(f"l1_ot{i}", [128, D], BF16) for i in range(2)]
            qob = [sb3(f"l1_qob{i}", [128, T], BF16) for i in range(2)]
            pt1 = st2.enter_context(nc.psum_tensor("l1_pt1", [128, 128], BF16))
            pt2 = st2.enter_context(nc.psum_tensor("l1_pt2", [128, 128], BF16))
            ps2 = [st2.enter_context(nc.psum_tensor(f"l1_ps2{i}", [128, 512], F32))
                   for i in range(2)]
            for dstb, wname in ((gbc, "l1_lng_row"), (bbc, "l1_lnb_row")):
                lg_ = p.dma("sync", grow[0:1, :], W[wname][:, :],
                            deps=(("e", "vector", p.cnt["vector"]),))
                evs = []
                for nh in range(2):
                    bc = mm_group(p, ps2[nh][:, :],
                                  [(c.ones_row_f[:, :],
                                    grow[0:1, nh * 512:(nh + 1) * 512])],
                                  deps=(lg_, ("e", "vector", p.cnt["vector"])))
                    evs.append(p.op("vector",
                                    lambda E, dstb=dstb, nh=nh, bank=ps2[nh]:
                                    E.tensor_copy(out=dstb[:, nh * 512:(nh + 1) * 512],
                                                  in_=bank[:, :]), deps=(bc,)))
                p.barrier()
            tr = Rot((pt1, pt2))
            prevt2 = None
            for ti in range(NT):
                t1 = p.op("vector", lambda E, ti=ti, xnt=c.xn_t:
                          E.scalar_tensor_tensor(
                    out=c.sc_f[:, 0:1024], in0=xnt[:, ti, :],
                    scalar=c.stats[:, ti, 2:3], in1=gbc[:, :],
                    op0=ALU.mult, op1=ALU.mult), deps=(prevt2,))
                t2 = p.op("vector", lambda E: E.tensor_tensor(
                    out=ln2t[:, :], in0=c.sc_f[:, 0:1024], in1=bbc[:, :],
                    op=ALU.add), deps=(t1,))
                for ch in range(NC):
                    bank, busy, j = tr.get()
                    tt = p.op("tensor", lambda E, ch=ch, bank=bank: E.transpose(
                        bank[:, :], ln2t[:, ch * 128:(ch + 1) * 128],
                        c.ident_bf[:, :]), deps=(busy, t2), inc=True)
                    prevt2 = p.op("vector", lambda E, ti=ti, ch=ch, bank=bank:
                                  E.tensor_copy(
                                      out=ln2ch[:, ch, ti * 128:(ti + 1) * 128],
                                      in_=bank[:, :]), deps=(tt,))
                    tr.set_busy(j, prevt2)
            p.barrier()
            rot = Rot(ps2)
            orot = Rot(otile)
            for wname, dram in (("l1_wk", k_dram), ("l1_wv", v_dram)):
                lw = p.dma("gpsimd", wkv[:, :, :],
                           W[wname].rearrange("(c p) n -> p c n", p=128),
                           deps=(("e", "tensor", p.cnt["tensor"]),))
                for ti in range(NT):
                    ob, obusy, oj = orot.get()
                    evs = []
                    for nh in range(2):
                        bank, busy, j = rot.get()
                        mm = mm_group(p, bank[:, :],
                                      [(ln2ch[:, kc, ti * 128:(ti + 1) * 128],
                                        wkv[:, kc, nh * 512:(nh + 1) * 512])
                                       for kc in range(NC)], deps=(lw, busy, obusy))
                        ev = p.op("vector", lambda E, nh=nh, bank=bank, ob=ob:
                                  E.tensor_copy(out=ob[:, nh * 512:(nh + 1) * 512],
                                                in_=bank[:, :]), deps=(mm,))
                        rot.set_busy(j, ev)
                        evs.append(ev)
                    dd = p.dma("scalar", dram[ti], ob[:, :], deps=tuple(evs))
                    orot.set_busy(oj, dd)
            qsrc = W["l1_wq"].rearrange("(c p) n -> p c n", p=128)
            wrot = Rot(wsl)
            qrot = Rot(qob)
            slots = {}
            for oc in range(2):
                buf, busy, j = wrot.get()
                slots[oc] = (buf, p.dma("gpsimd", buf[:, :, :],
                                        qsrc[:, :, oc * 128:(oc + 1) * 128],
                                        deps=(busy,)), j)
            for oc in range(NC):
                buf, lt, slot = slots.pop(oc)
                ob, obusy, oj = qrot.get()
                lastev = None
                for w4 in range(4):
                    bank, busy, j = rot.get()
                    mm = mm_group(p, bank[:, :],
                                  [(buf[:, kc, :],
                                    ln2ch[:, kc, w4 * 512:(w4 + 1) * 512])
                                   for kc in range(NC)], deps=(lt, busy, obusy))
                    lastev = p.op("vector", lambda E, w4=w4, bank=bank, ob=ob:
                                  E.tensor_copy(out=ob[:, w4 * 512:(w4 + 1) * 512],
                                                in_=bank[:, :]), deps=(mm,))
                    rot.set_busy(j, lastev)
                wrot.set_busy(slot, lastev)
                dq = p.dma("scalar", q_dram[oc], ob[:, :], deps=(lastev,))
                qrot.set_busy(oj, dq)
                if oc + 2 < NC:
                    buf2, busy2, j2 = wrot.get()
                    slots[oc + 2] = (buf2, p.dma(
                        "gpsimd", buf2[:, :, :],
                        qsrc[:, :, (oc + 2) * 128:(oc + 3) * 128], deps=(busy2,)), j2)
            p.barrier()
        with contextlib.ExitStack() as st2:
            sb3 = lambda n, s, d: st2.enter_context(nc.sbuf_tensor(n, s, d))
            ktl = [sb3(f"l1_ktl{i}", [128, D], BF16) for i in range(2)]
            vtl = [sb3(f"l1_vtl{i}", [128, D], BF16) for i in range(2)]
            gbb = sb3("l1_gbb", [64, 8, 128], F32)
            psGB = [st2.enter_context(nc.psum_tensor(f"l1_psGB{i}", [64, 128], F32))
                    for i in range(2)]
            krot, vrot2 = Rot(ktl), Rot(vtl)
            grot = Rot(psGB)
            prev_gbd = None
            for cc in range(NT):
                kb, kbusy, kj = krot.get()
                vb, vbusy, vj = vrot2.get()
                lk = p.dma("scalar", kb[:, :], k_dram[cc], deps=(kbusy,))
                lv = p.dma("scalar", vb[:, :], v_dram[cc], deps=(vbusy,))
                evs = []
                m2 = None
                for h in range(NH):
                    pr = cc * NH + h
                    bank, busy, j = grot.get()
                    p.op("tensor", lambda E, h=h, kb=kb, bank=bank:
                         E.matmul(bank[0:64, 0:64],
                                  kb[:, h * 64:(h + 1) * 64],
                                  kb[:, h * 64:(h + 1) * 64],
                                  start=True, stop=True), deps=(lk, lv, busy),
                         inc=False)
                    m2 = p.op("tensor", lambda E, h=h, kb=kb, vb=vb, bank=bank:
                              E.matmul(bank[0:64, 64:128],
                                       kb[:, h * 64:(h + 1) * 64],
                                       vb[:, h * 64:(h + 1) * 64],
                                       start=True, stop=True), deps=(),
                              inc=True)
                    ev1 = p.op("vector", lambda E, h=h, bank=bank:
                               E.scalar_tensor_tensor(
                                   out=gbb[0:64, h % 8, 0:64], in0=bank[0:64, 0:64],
                                   scalar=1.0, in1=eye4[:, :], op0=ALU.mult,
                                   op1=ALU.add), deps=(m2, g1, prev_gbd))
                    ev2 = p.op("vector", lambda E, h=h, bank=bank: E.tensor_copy(
                        out=gbb[0:64, h % 8, 64:128], in_=bank[0:64, 64:128]),
                        deps=(m2,))
                    grot.set_busy(j, ev2)
                    evs += [ev1, ev2]
                    if h % 8 == 7:
                        prev_gbd = p.dma("sync",
                                         gb_dram[pr - 7:pr + 1].rearrange(
                                             "q a b -> a q b"),
                                         gbb[0:64, :, :], deps=tuple(evs))
                        evs = []
                krot.set_busy(kj, m2)
                vrot2.set_busy(vj, m2)
            p.barrier()
        with contextlib.ExitStack() as st2:
            sb3 = lambda n, s, d: st2.enter_context(nc.sbuf_tensor(n, s, d))
            gjA = sb3("l1_gjA", [128, 64 * 128], F32)
            gjB = sb3("l1_gjB", [128, 64 * 128], F32)
            auxA = sb3("l1_auxA", [128, 66], F32)
            auxB = sb3("l1_auxB", [128, 66], F32)
            tmpA = sb3("l1_tmpA", [128, 16 * 128], F32)
            tmpB = sb3("l1_tmpB", [128, 16 * 128], F32)
            lA = p.dma("sync", gjA[:, :], gb_dram[0:128].rearrange("p a b -> p (a b)"))
            lB = p.dma("sync", gjB[:, :], gb_dram[128:256].rearrange("p a b -> p (a b)"))
            eA = emit_gj(p, "vector", gjA, auxA, tmpA, 64, 128, dep0=lA)
            eB = emit_gj(p, "gpsimd", gjB, auxB, tmpB, 64, 128, dep0=lB)
            p.dma("sync", w_dram[0:128],
                  gjA[:, :].rearrange("p (r l) -> p r l", l=128)[:, :, 64:128],
                  deps=(eA,))
            p.dma("sync", w_dram[128:256],
                  gjB[:, :].rearrange("p (r l) -> p r l", l=128)[:, :, 64:128],
                  deps=(eB,))
            p.barrier()
        o_t = sb2("l1_o_t", [128, NT, D], BF16)
        stN = stA.enter_context(contextlib.ExitStack())
        ntsb = stN.enter_context(nc.sbuf_tensor("l1_ntsb", [8, NPROB * 64], BF16))
        with contextlib.ExitStack() as st2:
            sb3 = lambda n, s, d: st2.enter_context(nc.sbuf_tensor(n, s, d))
            wbf = [sb3(f"l1_wbf{i}", [128, 4, 64], BF16) for i in range(2)]
            wtsb = sb3("l1_wtsb", [64, 64], BF16)
            wst = sb3("l1_wst", [64, 64], BF16)
            nsb = sb3("l1_nsb", [64, 8], BF16)
            crb = sb3("l1_crb", [8, 8, 72], F32)
            psW = st2.enter_context(nc.psum_tensor("l1_psW", [64, 64], BF16))
            psN2 = st2.enter_context(nc.psum_tensor("l1_psN2", [64, 64], F32))
            psC = st2.enter_context(nc.psum_tensor("l1_psC", [64, 128], F32))
            wsrc = w_dram.rearrange("(q a b) i j -> q (b i) a j", a=4, b=2)
            wrot = Rot(wbf)
            slots = {}
            for qq in range(2):
                buf, busy, j = wrot.get()
                slots[qq] = (buf, p.dma("gpsimd", buf[:, :, :], wsrc[qq],
                                        deps=(busy,)), j)
            prev_crd = None
            prevv = None
            for qq in range(32):
                buf, lt, slot = slots.pop(qq)
                lastop = None
                for ab in range(8):
                    a, b = ab // 2, ab % 2
                    pr = qq * 8 + a * 2 + b
                    wt = buf[b * 64:(b + 1) * 64, a, :]
                    t0 = p.op("vector", lambda E, wt=wt: E.tensor_copy(
                        out=wst[:, :], in_=wt), deps=(lt, prevv))
                    t1 = p.op("tensor", lambda E: E.transpose(
                        psW[:, :], wst[:, :], c.ident_bf[0:64, 0:64]),
                        deps=(t0,), inc=True)
                    t2 = p.op("vector", lambda E: E.tensor_copy(
                        out=wtsb[:, :], in_=psW[:, :]), deps=(t1,))
                    m1 = p.op("tensor", lambda E: E.matmul(
                        psN2[:, 0:8], wtsb[:, :], wst[:, 0:8], start=True,
                        stop=True), deps=(t2,), inc=True)
                    t3 = p.op("vector", lambda E: E.tensor_copy(
                        out=nsb[:, :], in_=psN2[:, 0:8]), deps=(m1,))
                    m2 = p.op("tensor", lambda E: E.matmul(
                        psC[0:8, 0:64], wst[:, 0:8], wtsb[:, :], start=True,
                        stop=True), deps=(t3,), inc=True)
                    t4 = p.op("vector", lambda E, pr=pr: E.tensor_copy(
                        out=ntsb[:, pr * 64:(pr + 1) * 64], in_=psC[0:8, 0:64]),
                        deps=(m2,))
                    m3 = p.op("tensor", lambda E: E.matmul(
                        psC[0:8, 64:72], nsb[:, :], nsb[:, :], start=True,
                        stop=True), deps=(t4,), inc=True)
                    m4 = p.op("tensor", lambda E: E.matmul(
                        psC[0:8, 0:64], nsb[:, :], wst[:, :], start=True,
                        stop=True), deps=(), inc=True)
                    t5 = p.op("vector", lambda E, ab=ab: E.tensor_copy(
                        out=crb[:, ab, 8:72], in_=psC[0:8, 0:64]),
                        deps=(m4, prev_crd))
                    prevv = p.op("vector", lambda E, ab=ab: E.tensor_copy(
                        out=crb[:, ab, 0:8], in_=psC[0:8, 64:72]),
                        deps=(m3, prev_crd))
                    lastop = prevv
                prev_crd = p.dma("sync", cr_dram[qq * 8:(qq + 1) * 8].rearrange(
                    "q a b -> a q b"), crb[:, :, :], deps=(lastop, t5))
                if qq + 2 < 32:
                    buf2, busy2, j2 = wrot.get()
                    slots[qq + 2] = (buf2, p.dma("gpsimd", buf2[:, :, :],
                                                 wsrc[qq + 2], deps=(busy2, lastop)),
                                     j2)
            p.barrier()
        with contextlib.ExitStack() as st2:
            sb3 = lambda n, s, d: st2.enter_context(nc.sbuf_tensor(n, s, d))
            gj2 = sb3("l1_gj2", [128, 2 * 8 * 72], F32)
            aux2 = sb3("l1_aux2", [128, 16], F32)
            tmp2 = sb3("l1_tmp2", [128, 16 * 72], F32)
            l2_ = p.dma("sync", gj2[:, :].rearrange("p (a i j) -> p a i j",
                                                     a=2, i=8),
                        cr_dram.rearrange("(a p) i j -> p a i j", p=128))
            ee = l2_
            for bb in range(2):
                ee = emit_gj(p, "vector", gj2, aux2, tmp2, 8, 72, off=bb * 576,
                             dep0=ee)
            for aa in range(2):
                p.dma("sync", x_dram[aa * 128:(aa + 1) * 128],
                      gj2[:, :].rearrange("p (a r l) -> p a r l", r=8, l=72)
                      [:, aa, :, 8:72], deps=(ee,))
            p.barrier()
        with contextlib.ExitStack() as st2:
            sb3 = lambda n, s, d: st2.enter_context(nc.sbuf_tensor(n, s, d))
            xcb = sb3("l1_xcb", [8, NH, 64], BF16)
            qsl = sb3("l1_qsl", [128, NC, 128], BF16)
            wrsb = sb3("l1_wrsb", [128, 8, 64], F32)
            ssb = sb3("l1_ssb", [128, 8, 64], F32)
            sbf = sb3("l1_sbf", [128, 8, 64], BF16)
            psWr = [st2.enter_context(nc.psum_tensor(f"l1_psWr{i}", [64, 64], F32))
                    for i in range(2)]
            psOu = [st2.enter_context(nc.psum_tensor(f"l1_psOu{i}", [128, 64], F32))
                    for i in range(2)]
            p.op("gpsimd", lambda E: E.memset(ssb[:, :, :], 0.0))
            p.barrier()
            wrot = Rot(psWr)
            orot = Rot(psOu)
            prev_x = None
            prev_q = None
            for cc in range(NT):
                lx = p.dma("gpsimd", xcb[:, :, :],
                           x_dram[cc * NH:(cc + 1) * NH].rearrange("h i j -> i h j"),
                           deps=(prev_x,))
                lq = p.dma("scalar", qsl[:, :, :],
                           q_dram.rearrange("a p t -> p a t")
                           [:, :, cc * 128:(cc + 1) * 128], deps=(prev_q,))
                wev = []
                for h in range(NH):
                    pr = cc * NH + h
                    bank, busy, j = wrot.get()
                    mm = p.op("tensor", lambda E, pr=pr, h=h, bank=bank: E.matmul(
                        bank[:, :], ntsb[:, pr * 64:(pr + 1) * 64],
                        xcb[:, h, :], start=True, stop=True), deps=(lx, busy),
                        inc=True)
                    hb = (h % 2) * 64
                    ev = p.op("vector", lambda E, h=h, hb=hb, bank=bank:
                              E.tensor_copy(out=wrsb[hb:hb + 64, h // 2, :],
                                            in_=bank[:, :]), deps=(mm,))
                    wrot.set_busy(j, ev)
                    wev.append(ev)
                su = p.op("vector", lambda E: E.scalar_tensor_tensor(
                    out=ssb[:, :, :], in0=ssb[:, :, :], scalar=0.9,
                    in1=wrsb[:, :, :], op0=ALU.mult, op1=ALU.add), deps=tuple(wev))
                scv = p.op("vector", lambda E: E.tensor_copy(
                    out=sbf[:, :, :], in_=ssb[:, :, :]), deps=(su,))
                oev = []
                for h in range(NH):
                    bank, busy, j = orot.get()
                    hb = (h % 2) * 64
                    mm = p.op("tensor", lambda E, h=h, hb=hb, bank=bank: E.matmul(
                        bank[:, :], qsl[hb:hb + 64, h // 2, :],
                        sbf[hb:hb + 64, h // 2, :], start=True, stop=True),
                        deps=(lq, scv, busy), inc=True)
                    ev = p.op("vector", lambda E, cc=cc, h=h, bank=bank:
                              E.tensor_scalar_mul(
                                  out=o_t[:, cc, h * 64:(h + 1) * 64],
                                  in0=bank[:, :], scalar1=0.125), deps=(mm,))
                    orot.set_busy(j, ev)
                    oev.append(ev)
                prev_x = oev[-1]
                prev_q = oev[-1]
            p.barrier()
        stN.close()
        with contextlib.ExitStack() as st2:
            sb3 = lambda n, s, d: st2.enter_context(nc.sbuf_tensor(n, s, d))
            och = sb3("l1_och", [128, NC, T], BF16)
            wo = sb3("l1_wo_sb", [128, NC, D], BF16)
            stl = [sb3(f"l1_stl{i}", [128, D], BF16) for i in range(2)]
            pt1 = st2.enter_context(nc.psum_tensor("l1_pt1b", [128, 128], BF16))
            pt2 = st2.enter_context(nc.psum_tensor("l1_pt2b", [128, 128], BF16))
            ps2 = [st2.enter_context(nc.psum_tensor(f"l1_ps2b{i}", [128, 512], F32))
                   for i in range(2)]
            lwo = p.dma("gpsimd", wo[:, :, :],
                        W["l1_wo"].rearrange("(c p) n -> p c n", p=128))
            emit_t2ch(p, c, o_t, och, (pt1, pt2), col0=0)
            p.barrier()
            rot = Rot(ps2)
            srot = Rot(stl)
            for ti in range(NT):
                sb_, sbusy, sj = srot.get()
                evs = []
                for nh in range(2):
                    bank, busy, j = rot.get()
                    mm = mm_group(p, bank[:, :],
                                  [(och[:, ch, ti * 128:(ti + 1) * 128],
                                    wo[:, ch, nh * 512:(nh + 1) * 512])
                                   for ch in range(NC)], deps=(lwo, busy, sbusy))
                    ev = p.op("vector",
                              lambda E, ti=ti, nh=nh, bank=bank, sb_=sb_, xnt=c.xn_t:
                              E.tensor_tensor(out=sb_[:, nh * 512:(nh + 1) * 512],
                                              in0=bank[:, :],
                                              in1=xnt[:, ti, nh * 512:(nh + 1) * 512],
                                              op=ALU.add), deps=(mm,))
                    rot.set_busy(j, ev)
                    evs.append(ev)
                sd = p.dma("sync", s_dram[ti], sb_[:, :], deps=tuple(evs))
                srot.set_busy(sj, sd)
            p.barrier()

    with contextlib.ExitStack() as stF:
        sb2 = lambda n, s, d: stF.enter_context(nc.sbuf_tensor(n, s, d))
        xn_ch = sb2("l1_xn_ch", [128, NC, T], BF16)
        gacc = sb2("l1_gacc", [128, NT, D], F32)
        cw = sb2("l1_cw", [128, NT, 8], F32)
        gz = p.op("gpsimd", lambda E: E.memset(gacc[:, :, :], 0.0))
        with contextlib.ExitStack() as st2:
            c.xn_t = st2.enter_context(nc.sbuf_tensor("l1_xn_t2", [128, NT, D], BF16))
            pt1 = st2.enter_context(nc.psum_tensor("l1_pt1c", [128, 128], BF16))
            pt2 = st2.enter_context(nc.psum_tensor("l1_pt2c", [128, 128], BF16))
            emit_ln(p, c)
            p.barrier()
            emit_t2ch(p, c, c.xn_t, xn_ch, (pt1, pt2), col0=0)
            p.barrier()
        with contextlib.ExitStack() as st2:
            rw = st2.enter_context(nc.sbuf_tensor("l1_rw", [128, NC, 8], BF16))
            psG = st2.enter_context(nc.psum_tensor("l1_psG", [128, 512], F32))
            lr = p.dma("gpsimd", rw[:, :, :],
                       W["l1_rwT"].rearrange("(c p) n -> p c n", p=128))
            sc = c.sc_f
            prev = None
            for ti in range(NT):
                mm = mm_group(p, psG[:, 0:8],
                              [(xn_ch[:, kc, ti * 128:(ti + 1) * 128], rw[:, kc, :])
                               for kc in range(NC)],
                              deps=(lr,) if prev is None else (lr, prev))
                t1 = p.op("vector", lambda E: E.tensor_reduce(
                    out=sc[:, 8:9], in_=psG[:, 0:8], axis=AX, op=ALU.max),
                    deps=(mm,))
                t1b = p.op("vector", lambda E: E.tensor_scalar_mul(
                    out=sc[:, 9:10], in0=sc[:, 8:9], scalar1=-1.0))
                t2 = p.op("scalar", lambda E: E.activation(
                    out=sc[:, 0:8], in_=psG[:, 0:8], func=AF.Exp,
                    bias=sc[:, 9:10], accum_out=sc[:, 10:11]), deps=(t1b,))
                t3 = p.op("vector", lambda E: E.reciprocal(
                    out=sc[:, 11:12], in_=sc[:, 10:11]), deps=(t2,))
                t4 = p.op("vector", lambda E: E.tensor_scalar_mul(
                    out=sc[:, 0:8], in0=sc[:, 0:8], scalar1=sc[:, 11:12]))
                t5 = p.op("vector", lambda E: E.tensor_reduce(
                    out=sc[:, 12:13], in_=sc[:, 0:8], axis=AX, op=ALU.max))
                te = p.op("vector", lambda E: E.tensor_scalar(
                    out=sc[:, 16:24], in0=sc[:, 0:8], scalar1=sc[:, 12:13],
                    op0=ALU.is_ge, scalar2=None))
                tm = p.op("vector", lambda E: E.scalar_tensor_tensor(
                    out=sc[:, 24:32], in0=sc[:, 16:24], scalar=-2.0,
                    in1=sc[:, 0:8], op0=ALU.mult, op1=ALU.add))
                tn = p.op("vector", lambda E: E.tensor_reduce(
                    out=sc[:, 13:14], in_=sc[:, 24:32], axis=AX, op=ALU.max))
                tk = p.op("vector", lambda E: E.tensor_scalar(
                    out=sc[:, 32:40], in0=sc[:, 0:8], scalar1=sc[:, 13:14],
                    op0=ALU.is_ge, scalar2=None))
                prev = p.op("vector", lambda E, ti=ti: E.tensor_tensor(
                    out=cw[:, ti, :], in0=sc[:, 0:8], in1=sc[:, 32:40],
                    op=ALU.mult))
            p.barrier()
        with contextlib.ExitStack() as st2:
            sb3 = lambda n, s, d: st2.enter_context(nc.sbuf_tensor(n, s, d))
            w1sl = [sb3(f"l1_w1sl{i}", [128, NC, 128], BF16) for i in range(4)]
            w2sl = [sb3(f"l1_w2sl{i}", [128, 512], BF16) for i in range(2)]
            ywin = sb3("l1_ywin", [128, 16, 512], BF16)
            gtmp = sb3("l1_gtmp", [128, 512], F32)
            psY = [st2.enter_context(nc.psum_tensor(f"l1_psY{i}", [128, 512], F32))
                   for i in range(2)]
            psO4 = [st2.enter_context(nc.psum_tensor(f"l1_psO4{i}", [128, 512], F32))
                    for i in range(4)]
            w1src = W["l1_w1T"].rearrange("e (c p) n -> e p c n", p=128)
            w2src = W["l1_w2T"].rearrange("e (c p) n -> e p c n", p=128)
            wrot = Rot(w1sl)
            w2rot = Rot(w2sl)
            prevga = None
            prev_w2 = None
            for e in range(8):
                for w4 in range(4):
                    slots = {}
                    for i2 in range(2):
                        bufg, busyg, jg = wrot.get()
                        bufu, busyu, ju = wrot.get()
                        lg_ = p.dma("gpsimd", bufg[:, :, :],
                                    w1src[e][:, :, i2 * 128:(i2 + 1) * 128],
                                    deps=(busyg,))
                        lu_ = p.dma("gpsimd", bufu[:, :, :],
                                    w1src[e][:, :, 2048 + i2 * 128:
                                             2048 + (i2 + 1) * 128],
                                    deps=(busyu,))
                        slots[i2] = (bufg, bufu, lg_, lu_, jg, ju)
                    yrot = Rot(psY)
                    ym_toks = {}
                    for i in range(16):
                        bufg, bufu, lg_, lu_, jg, ju = slots.pop(i)
                        bg, bgbusy, jb = yrot.get()
                        bu, bubusy, jb2 = yrot.get()
                        mmg = mm_group(p, bg[:, :],
                                       [(bufg[:, kc, :],
                                         xn_ch[:, kc, w4 * 512:(w4 + 1) * 512])
                                        for kc in range(NC)], deps=(lg_, bgbusy))
                        mmu = mm_group(p, bu[:, :],
                                       [(bufu[:, kc, :],
                                         xn_ch[:, kc, w4 * 512:(w4 + 1) * 512])
                                        for kc in range(NC)], deps=(lu_, bubusy))
                        sl = p.op("scalar", lambda E, bg=bg: E.activation(
                            out=gtmp[:, :], in_=bg[:, :], func=AF.Silu),
                            deps=(mmg,))
                        ym = p.op("vector", lambda E, i=i, bu=bu: E.tensor_tensor(
                            out=ywin[:, i, :], in0=bu[:, :], in1=gtmp[:, :],
                            op=ALU.mult), deps=(sl, mmu, prev_w2))
                        ym_toks[i] = ym
                        yrot.set_busy(jb, ym)
                        yrot.set_busy(jb2, ym)
                        wrot.set_busy(jg, mmg)
                        wrot.set_busy(ju, mmu)
                        if i + 2 < 16:
                            bufg2, busyg2, jg2 = wrot.get()
                            bufu2, busyu2, ju2 = wrot.get()
                            lg2 = p.dma("gpsimd", bufg2[:, :, :],
                                        w1src[e][:, :, (i + 2) * 128:(i + 3) * 128],
                                        deps=(busyg2,))
                            lu2 = p.dma("gpsimd", bufu2[:, :, :],
                                        w1src[e][:, :, 2048 + (i + 2) * 128:
                                                 2048 + (i + 3) * 128],
                                        deps=(busyu2,))
                            slots[i + 2] = (bufg2, bufu2, lg2, lu2, jg2, ju2)
                    for dgrp in range(2):
                        wslots = {}
                        for i2 in range(2):
                            buf, busy, j = w2rot.get()
                            wslots[i2] = (buf, p.dma(
                                "gpsimd", buf[:, :],
                                w2src[e][:, i2, dgrp * 512:dgrp * 512 + 512],
                                deps=(busy,)), j)
                        lastmm = None
                        for i in range(16):
                            buf, lt, slot = wslots.pop(i)
                            for tc in range(4):
                                lastmm = p.op(
                                    "tensor",
                                    lambda E, buf=buf, i=i, tc=tc,
                                    s=(i == 0), e2=(i == 15):
                                    E.matmul(psO4[tc][:, :],
                                             ywin[:, i, tc * 128:(tc + 1) * 128],
                                             buf[:, :], start=s, stop=e2,
                                             skip_group_check=True),
                                    deps=(lt, ym_toks[i], prevga)
                                    if tc == 0 else (),
                                    inc=(tc == 3))
                            w2rot.set_busy(slot, lastmm)
                            if i + 2 < 16:
                                buf2, busy2, j2 = w2rot.get()
                                wslots[i + 2] = (buf2, p.dma(
                                    "gpsimd", buf2[:, :],
                                    w2src[e][:, i + 2, dgrp * 512:dgrp * 512 + 512],
                                    deps=(busy2,)), j2)
                        prev_w2 = lastmm
                        for tc in range(4):
                            ti = w4 * 4 + tc
                            prevga = p.op(
                                "vector",
                                lambda E, tc=tc, ti=ti, e=e, dgrp=dgrp:
                                E.scalar_tensor_tensor(
                                    out=gacc[:, ti, dgrp * 512:(dgrp + 1) * 512],
                                    in0=psO4[tc][:, :], scalar=cw[:, ti, e:e + 1],
                                    in1=gacc[:, ti, dgrp * 512:(dgrp + 1) * 512],
                                    op0=ALU.mult, op1=ALU.add),
                                deps=(lastmm, gz, prevga))
                        prev_w2 = prevga
            p.barrier()
        with contextlib.ExitStack() as st2:
            stl = [st2.enter_context(nc.sbuf_tensor(f"l1_stl2{i}", [128, D], BF16))
                   for i in range(2)]
            srot = Rot(stl)
            prev_add = None
            for ti in range(NT):
                sb_, sbusy, sj = srot.get()
                sld = p.dma("sync", sb_[:, :], s_dram[ti], deps=(sbusy,))
                for nh in range(2):
                    g1_ = p.op("scalar", lambda E, ti=ti, nh=nh: E.activation(
                        out=c.sc_f[:, 0:512],
                        in_=gacc[:, ti, nh * 512:(nh + 1) * 512],
                        func=AF.Silu), deps=(prev_add,))
                    g2_ = p.op("vector", lambda E, nh=nh, sb_=sb_: E.tensor_tensor(
                        out=c.sc_f2[:, 0:512], in0=c.sc_f[:, 0:512],
                        in1=sb_[:, nh * 512:(nh + 1) * 512], op=ALU.mult),
                        deps=(g1_, sld))
                    prev_add = p.op("vector", lambda E, ti=ti, nh=nh: E.tensor_tensor(
                        out=c.x[:, ti, nh * 512:(nh + 1) * 512],
                        in0=c.x[:, ti, nh * 512:(nh + 1) * 512],
                        in1=c.sc_f2[:, 0:512], op=ALU.add), deps=(g2_,))
                srot.set_busy(sj, prev_add)
            p.barrier()



def emit_layer2(p, c, W, stack):
    nc = p.nc
    sb = lambda name, shape, dt: stack.enter_context(nc.sbuf_tensor(name, shape, dt))
    NH = 16
    s_dram = nc.dram_tensor("l2_s_scr", [NT, 128, D], BF16)
    knch_d = nc.dram_tensor("l2_knch_scr", [NC, 128, T], BF16)
    ktch_d = nc.dram_tensor("l2_ktch_scr", [NC, 128, T], BF16)
    qch_d = nc.dram_tensor("l2_qch_scr", [NC, 128, T], BF16)
    knt_d = nc.dram_tensor("l2_knt_scr", [NT, 128, D], BF16)
    kbt_d = nc.dram_tensor("l2_kbt_scr", [NT, 128, D], BF16)
    vt_d = nc.dram_tensor("l2_vt_scr", [NT, 128, D], BF16)

    stTop = contextlib.ExitStack()
    sb = lambda name, shape, dt: stTop.enter_context(nc.sbuf_tensor(name, shape, dt))
    m01 = sb("l2_m01", [128, 3, 128], BF16)
    gm = p.op("gpsimd", lambda E: E.memset(m01[:, :, :], 1.0))
    gm = p.op("gpsimd", lambda E: E.affine_select(
        out=m01[:, 0, :], in_=m01[:, 0, :], compare_op=ALU.is_gt, fill=0.0,
        base=0, pattern=[[-1, 128]], channel_multiplier=1), deps=(gm,))
    gm = p.op("gpsimd", lambda E: E.affine_select(
        out=m01[:, 1, :], in_=m01[:, 1, :], compare_op=ALU.is_gt, fill=0.0,
        base=0, pattern=[[1, 128]], channel_multiplier=-1), deps=(gm,))
    gm = p.op("gpsimd", lambda E: E.affine_select(
        out=m01[:, 2, :], in_=m01[:, 2, :], compare_op=ALU.is_ge, fill=0.0,
        base=0, pattern=[[1, 128]], channel_multiplier=-1), deps=(gm,))

    x2_t = sb("l2_x2", [128, NT, D], BF16)
    beta = sb("l2_beta", [128, NT, 16], F32)
    with contextlib.ExitStack() as stA:
        sb2 = lambda n, s, d: stA.enter_context(nc.sbuf_tensor(n, s, d))
        c.xn_t = sb2("l2_xn_t", [128, NT, D], BF16)
        emit_ln(p, c)
        p.barrier()
        for ti in range(NT):
            st = c.stats[:, ti, :]
            a1 = p.op("vector", lambda E, st=st: E.tensor_scalar(
                out=st[:, 5:6], in0=st[:, 1:2], scalar1=st[:, 1:2],
                scalar2=st[:, 6:7], op0=ALU.mult, op1=ALU.mult))
            a2 = p.op("scalar", lambda E, st=st: E.activation(
                out=st[:, 7:8], in_=st[:, 5:6], func=AF.Sqrt,
                bias=c.cst[:, 1:2]), deps=(a1,))
            p.op("vector", lambda E, st=st: E.reciprocal(
                out=st[:, 3:4], in_=st[:, 7:8]), deps=(a2,))
        p.barrier()
        with contextlib.ExitStack() as st2:
            sb3 = lambda n, s, d: st2.enter_context(nc.sbuf_tensor(n, s, d))
            h1ch = sb3("l2_h1ch", [128, NC, T], BF16)
            grow = c.sc_f
            brow = sb3("l2_brow", [1, D], BF16)
            b16r = sb3("l2_b16r", [1, 16], BF16)
            bqcol = sb3("l2_bqcol", [128, 8], F32)
            ktile = sb3("l2_ktile", [128, D], F32)
            kntile = sb3("l2_kntile", [128, D], BF16)
            kbtile = sb3("l2_kbtile", [128, D], BF16)
            rcol = sb3("l2_rcol", [128, 16], F32)
            wbeta_sb = sb3("l2_wbeta_sb", [128, NC, 16], BF16)
            stP = contextlib.ExitStack()
            lnb1 = stP.enter_context(nc.sbuf_tensor("l2_lnb1", [128, D], F32))
            h1t = stP.enter_context(nc.sbuf_tensor("l2_h1t", [128, 1024], BF16))
            pt1 = st2.enter_context(nc.psum_tensor("l2_pt1", [128, 128], BF16))
            pt2 = st2.enter_context(nc.psum_tensor("l2_pt2", [128, 128], BF16))
            ps2 = [st2.enter_context(nc.psum_tensor(f"l2_ps2{i}", [128, 512], F32))
                   for i in range(2)]
            psB = st2.enter_context(nc.psum_tensor("l2_psB", [128, 512], F32))
            lg_ = p.dma("sync", grow[0:1, :], W["l2_ln1_row"][:, :])
            lqbc = p.dma("sync", bqcol[:, :], W["l2_bq_col"][:, :])
            for nh in range(2):
                bc = mm_group(p, ps2[nh][:, :],
                              [(c.ones_row_f[:, :], grow[0:1, nh * 512:(nh + 1) * 512])],
                              deps=(lg_,))
                p.op("vector", lambda E, nh=nh, bank=ps2[nh]:
                     E.tensor_copy(out=lnb1[:, nh * 512:(nh + 1) * 512],
                                   in_=bank[:, :]), deps=(bc,))
            p.barrier()
            tr = Rot((pt1, pt2))
            prevt = None
            for ti in range(NT):
                t1 = p.op("vector", lambda E, ti=ti, xnt=c.xn_t, h1t=h1t,
                          lnb1=lnb1: E.scalar_tensor_tensor(
                    out=h1t[:, :], in0=xnt[:, ti, :], scalar=c.stats[:, ti, 3:4],
                    in1=lnb1[:, :], op0=ALU.mult, op1=ALU.mult), deps=(prevt,))
                for ch in range(NC):
                    bank, busy, j = tr.get()
                    tt = p.op("tensor", lambda E, ch=ch, bank=bank, h1t=h1t:
                              E.transpose(
                        bank[:, :], h1t[:, ch * 128:(ch + 1) * 128],
                        c.ident_bf[:, :]), deps=(busy, t1), inc=True)
                    prevt = p.op("vector", lambda E, ti=ti, ch=ch, bank=bank:
                                 E.tensor_copy(out=h1ch[:, ch, ti * 128:(ti + 1) * 128],
                                               in_=bank[:, :]), deps=(tt,))
                    tr.set_busy(j, prevt)
            p.barrier()
            stP.close()
            lwb = p.dma("gpsimd", wbeta_sb[:, :, :],
                        W["l2_wbeta"].rearrange("(c p) n -> p c n", p=128))
            lbb = p.dma("gpsimd", b16r[:, :], W["l2_bbeta_row"][:, :])
            prevb = None
            for ti in range(NT):
                pairs = [(h1ch[:, kc, ti * 128:(ti + 1) * 128], wbeta_sb[:, kc, :])
                         for kc in range(NC)]
                pairs.append((c.ones_row_bf[:, :], b16r[0:1, :]))
                mm = mm_group(p, psB[:, 0:16], pairs, deps=(lwb, lbb, prevb))
                prevb = p.op("scalar", lambda E, ti=ti: E.activation(
                    out=beta[:, ti, :], in_=psB[:, 0:16], func=AF.Sigmoid),
                    deps=(mm,))
            p.barrier()
            stKV = contextlib.ExitStack()
            wbuf = stKV.enter_context(nc.sbuf_tensor("l2_wbuf", [128, NC, D], BF16))
            otile = [stKV.enter_context(nc.sbuf_tensor(f"l2_otile{i}", [128, D], BF16))
                     for i in range(3)]
            rot = Rot(ps2)
            orot = Rot(otile)
            trr = Rot((pt1, pt2))
            prevreads = ()
            for wname, bname in (("l2_wk", "l2_bk_row"), ("l2_wv", "l2_bv_row")):
                is_k = wname == "l2_wk"
                lw = p.dma("gpsimd", wbuf[:, :, :],
                           W[wname].rearrange("(c p) n -> p c n", p=128),
                           deps=(("e", "tensor", p.cnt["tensor"]),))
                lb = p.dma("gpsimd", brow[:, :], W[bname][:, :])
                for ti in range(NT):
                    kevs = []
                    for nh in range(2):
                        bank, busy, j = rot.get()
                        pairs = [(h1ch[:, kc, ti * 128:(ti + 1) * 128],
                                  wbuf[:, kc, nh * 512:(nh + 1) * 512])
                                 for kc in range(NC)]
                        pairs.append((c.ones_row_bf[:, :],
                                      brow[0:1, nh * 512:(nh + 1) * 512]))
                        mm = mm_group(p, bank[:, :], pairs, deps=(lw, lb, busy))
                        kevs.append(p.op("vector", lambda E, nh=nh, bank=bank:
                                         E.tensor_copy(
                                             out=ktile[:, nh * 512:(nh + 1) * 512],
                                             in_=bank[:, :]), deps=(mm,)))
                        rot.set_busy(j, kevs[-1])
                    if is_k:
                        sq = p.op("scalar", lambda E: E.activation(
                            out=c.sc_f[:, 0:1024], in_=ktile[:, :], func=AF.Square),
                            deps=tuple(kevs))
                        rd = p.op("vector", lambda E: E.tensor_reduce(
                            out=rcol[:, 0:16],
                            in_=c.sc_f[:, 0:1024].rearrange("p (g n) -> p g n", g=16),
                            axis=AX, op=ALU.add), deps=(sq,))
                        s2 = p.op("scalar", lambda E: E.activation(
                            out=rcol[:, 0:16], in_=rcol[:, 0:16], func=AF.Sqrt),
                            deps=(rd,))
                        s3 = p.op("vector", lambda E: E.tensor_scalar_add(
                            out=rcol[:, 0:16], in0=rcol[:, 0:16], scalar1=1e-6),
                            deps=(s2,))
                        s4 = p.op("vector", lambda E: E.reciprocal(
                            out=rcol[:, 0:16], in_=rcol[:, 0:16]))
                        kn = p.op("vector", lambda E: E.tensor_tensor(
                            out=kntile[:, :].rearrange("p (g n) -> p g n", g=16),
                            in0=ktile[:, :].rearrange("p (g n) -> p g n", g=16),
                            in1=rcol[:, 0:16].rearrange("p (g o) -> p g o", o=1)
                            .broadcast_to([128, 16, 64]), op=ALU.mult),
                            deps=prevreads)
                        kb = p.op("vector", lambda E, ti=ti: E.tensor_tensor(
                            out=kbtile[:, :].rearrange("p (g n) -> p g n", g=16),
                            in0=kntile[:, :].rearrange("p (g n) -> p g n", g=16),
                            in1=beta[:, ti, :].rearrange("p (g o) -> p g o", o=1)
                            .broadcast_to([128, 16, 64]), op=ALU.mult),
                            deps=(kn, prevb))
                        d1 = p.dma("scalar", knt_d[ti], kntile[:, :], deps=(kn,))
                        d2 = p.dma("scalar", kbt_d[ti], kbtile[:, :], deps=(kb,))
                        for which, srct, dramt in ((0, kntile, knch_d),
                                                   (1, kbtile, ktch_d)):
                            ob, obusy, oj = orot.get()
                            lastv = kb
                            for ch in range(NC):
                                bank, busy, j = trr.get()
                                tt = p.op("tensor",
                                          lambda E, srct=srct, ch=ch, bank=bank:
                                          E.transpose(bank[:, :],
                                                      srct[:, ch * 128:(ch + 1) * 128],
                                                      c.ident_bf[:, :]),
                                          deps=(busy, lastv, obusy), inc=True)
                                lastv = p.op("vector",
                                             lambda E, ch=ch, bank=bank, ob=ob:
                                             E.tensor_copy(
                                                 out=ob[:, ch * 128:(ch + 1) * 128],
                                                 in_=bank[:, :]), deps=(tt,))
                                trr.set_busy(j, lastv)
                            dd = p.dma("scalar",
                                       dramt.rearrange("a p t -> p a t")
                                       [:, :, ti * 128:(ti + 1) * 128],
                                       ob[:, :].rearrange("p (a n) -> p a n", a=NC),
                                       deps=(lastv,))
                            orot.set_busy(oj, dd)
                        prevreads = (d1, d2, lastv)
                    else:
                        vb = p.op("vector", lambda E, ti=ti: E.tensor_tensor(
                            out=kbtile[:, :].rearrange("p (g n) -> p g n", g=16),
                            in0=ktile[:, :].rearrange("p (g n) -> p g n", g=16),
                            in1=beta[:, ti, :].rearrange("p (g o) -> p g o", o=1)
                            .broadcast_to([128, 16, 64]), op=ALU.mult),
                            deps=tuple(kevs) + (prevb,) + prevreads)
                        dv = p.dma("scalar", vt_d[ti], kbtile[:, :], deps=(vb,))
                        prevreads = (dv,)
            p.barrier()
            stKV.close()
            stQ = contextlib.ExitStack()
            wsl = [stQ.enter_context(nc.sbuf_tensor(f"l2_wsl{i}", [128, NC, 128], BF16))
                   for i in range(2)]
            qob = [stQ.enter_context(nc.sbuf_tensor(f"l2_qob{i}", [128, T], BF16))
                   for i in range(2)]
            qsrc = W["l2_wq"].rearrange("(c p) n -> p c n", p=128)
            wrot = Rot(wsl)
            qrot = Rot(qob)
            slots = {}
            for oc in range(2):
                buf, busy, j = wrot.get()
                slots[oc] = (buf, p.dma("gpsimd", buf[:, :, :],
                                        qsrc[:, :, oc * 128:(oc + 1) * 128],
                                        deps=(busy,)), j)
            for oc in range(NC):
                buf, lt, slot = slots.pop(oc)
                ob, obusy, oj = qrot.get()
                lastev = None
                for w4 in range(4):
                    bank, busy, j = rot.get()
                    pairs = [(buf[:, kc, :], h1ch[:, kc, w4 * 512:(w4 + 1) * 512])
                             for kc in range(NC)]
                    mm = mm_group(p, bank[:, :], pairs, deps=(lt, lqbc, busy, obusy))
                    lastev = p.op("scalar", lambda E, w4=w4, bank=bank, ob=ob, oc=oc:
                                  E.activation(out=ob[:, w4 * 512:(w4 + 1) * 512],
                                               in_=bank[:, :], func=AF.Identity,
                                               bias=bqcol[:, oc:oc + 1]),
                                  deps=(mm,))
                    rot.set_busy(j, lastev)
                wrot.set_busy(slot, lastev)
                dq = p.dma("scalar", qch_d[oc], ob[:, :], deps=(lastev,))
                qrot.set_busy(oj, dq)
                if oc + 2 < NC:
                    buf2, busy2, j2 = wrot.get()
                    slots[oc + 2] = (buf2, p.dma(
                        "gpsimd", buf2[:, :, :],
                        qsrc[:, :, (oc + 2) * 128:(oc + 3) * 128], deps=(busy2,)), j2)
            p.barrier()
            stQ.close()
        o_t = sb2("l2_o_t", [128, NT, D], BF16)
        with contextlib.ExitStack() as st2:
            sb3 = lambda n, s, d: st2.enter_context(nc.sbuf_tensor(n, s, d))
            knc = [sb3(f"l2_knc{i}", [128, NC, 128], BF16) for i in range(2)]
            ktc = [sb3(f"l2_ktc{i}", [128, NC, 128], BF16) for i in range(2)]
            qcc = [sb3(f"l2_qcc{i}", [128, NC, 128], BF16) for i in range(2)]
            kntb = [sb3(f"l2_kntb{i}", [128, D], BF16) for i in range(2)]
            kbtb = [sb3(f"l2_kbtb{i}", [128, D], BF16) for i in range(2)]
            vtb2 = [sb3(f"l2_vtb{i}", [128, D], BF16) for i in range(2)]
            alv = [sb3(f"l2_alv{i}", [128, 128], BF16) for i in range(7)]
            aro = [sb3(f"l2_aro{i}", [128, 128], BF16) for i in range(2)]
            xbu = [sb3(f"l2_xb{i}", [128, 128], BF16) for i in range(2)]
            wTb = sb3("l2_wTb", [128, 128], BF16)
            upp = sb3("l2_upp", [128, 64], BF16)
            mpr = sb3("l2_mpr", [128, 128], BF16)
            Ssb = sb3("l2_Ssb", [128, 8, 64], F32)
            Sbf = sb3("l2_Sbf", [128, 8, 64], BF16)
            psA2 = [st2.enter_context(nc.psum_tensor(f"l2_psA2{i}", [128, 128], F32))
                    for i in range(2)]
            psX = st2.enter_context(nc.psum_tensor("l2_psX", [128, 128], F32))
            psT3 = st2.enter_context(nc.psum_tensor("l2_psT3", [64, 128], BF16))
            psS2 = st2.enter_context(nc.psum_tensor("l2_psS2", [64, 64], F32))
            psO2 = st2.enter_context(nc.psum_tensor("l2_psO2", [128, 64], F32))
            zz = p.op("gpsimd", lambda E: E.memset(Ssb[:, :, :], 0.0))
            p.op("gpsimd", lambda E: E.memset(Sbf[:, :, :], 0.0), deps=(zz,))
            p.barrier()
            k_rot, kt_rot, q_rot = Rot(knc), Rot(ktc), Rot(qcc)
            kn_rot, kb_rot, v_rot = Rot(kntb), Rot(kbtb), Rot(vtb2)
            arot2 = Rot(psA2)
            xrot = Rot(xbu)
            evacE = ["vector", "scalar"]
            for cc in range(NT):
                kcb, kcbusy, kcj = k_rot.get()
                ktb, ktbusy, ktj = kt_rot.get()
                qcb, qcbusy, qcj = q_rot.get()
                knb, knbusy, knj = kn_rot.get()
                kbb, kbbusy, kbj = kb_rot.get()
                vtbb, vtbusy, vtj = v_rot.get()
                lk1 = p.dma("scalar", kcb[:, :, :],
                            knch_d.rearrange("a p t -> p a t")
                            [:, :, cc * 128:(cc + 1) * 128], deps=(kcbusy,))
                lk2 = p.dma("scalar", ktb[:, :, :],
                            ktch_d.rearrange("a p t -> p a t")
                            [:, :, cc * 128:(cc + 1) * 128], deps=(ktbusy,))
                lk3 = p.dma("scalar", qcb[:, :, :],
                            qch_d.rearrange("a p t -> p a t")
                            [:, :, cc * 128:(cc + 1) * 128], deps=(qcbusy,))
                lk4 = p.dma("sync", knb[:, :], knt_d[cc], deps=(knbusy,))
                lk5 = p.dma("sync", kbb[:, :], kbt_d[cc], deps=(kbbusy,))
                lk6 = p.dma("sync", vtbb[:, :], vt_d[cc], deps=(vtbusy,))
                lasth = None
                for h in range(NH):
                    hb = (h % 2) * 64
                    hc = h // 2
                    ksl = kcb[hb:hb + 64, hc, :]
                    ktsl = ktb[hb:hb + 64, hc, :]
                    qsl = qcb[hb:hb + 64, hc, :]
                    eng = evacE[h % 2]
                    bank, busy, j = arot2.get()
                    mA = p.op("tensor", lambda E, ktsl=ktsl, ksl=ksl, bank=bank:
                              E.matmul(bank[:, :], ktsl, ksl, start=True, stop=True),
                              deps=(lk1, lk2, busy), inc=True)
                    evA = p.op("vector", lambda E, bank=bank: E.tensor_tensor(
                        out=aro[0][:, :], in0=bank[:, :], in1=m01[:, 0, :],
                        op=ALU.mult), deps=(mA, lasth))
                    arot2.set_busy(j, evA)
                    bank2, busy2, j2 = arot2.get()
                    mAt = p.op("tensor", lambda E, ktsl=ktsl, ksl=ksl, bank=bank2:
                               E.matmul(bank[:, :], ksl, ktsl, start=True, stop=True),
                               deps=(busy2,), inc=True)
                    evAt = p.op("vector", lambda E, bank=bank2: E.tensor_tensor(
                        out=alv[0][:, :], in0=bank[:, :], in1=m01[:, 1, :],
                        op=ALU.mult), deps=(mAt,))
                    arot2.set_busy(j2, evAt)
                    acur, atprev = aro[0], alv[0]
                    prevlev = (evA, evAt)
                    for lv in range(1, 7):
                        bank, busy, j = arot2.get()
                        mN = p.op("tensor", lambda E, atprev=atprev, acur=acur,
                                  bank=bank:
                                  E.matmul(bank[:, :], atprev[:, :], acur[:, :],
                                           start=True, stop=True),
                                  deps=(busy,) + prevlev, inc=True)
                        anew = aro[1] if acur is aro[0] else aro[0]
                        eA = p.op("vector", lambda E, bank=bank, anew=anew:
                                  E.tensor_copy(out=anew[:, :], in_=bank[:, :]),
                                  deps=(mN,))
                        arot2.set_busy(j, eA)
                        bank2, busy2, j2 = arot2.get()
                        mNt = p.op("tensor", lambda E, atprev=atprev, acur=acur,
                                   bank=bank2:
                                   E.matmul(bank[:, :], acur[:, :], atprev[:, :],
                                            start=True, stop=True),
                                   deps=(busy2,), inc=True)
                        eAt = p.op("scalar", lambda E, bank=bank2, lv=lv:
                                   E.activation(out=alv[lv][:, :], in_=bank[:, :],
                                                func=AF.Identity), deps=(mNt,))
                        arot2.set_busy(j2, eAt)
                        acur, atprev = anew, alv[lv]
                        prevlev = (eA, eAt)
                    xc, xbusy, xj = xrot.get()
                    c1 = p.op("vector", lambda E, kbb=kbb, h=h, xc=xc: E.tensor_copy(
                        out=xc[:, 0:64], in_=kbb[:, h * 64:(h + 1) * 64]),
                        deps=(lk5, xbusy) + prevlev)
                    c2 = p.op("vector", lambda E, vtbb=vtbb, h=h, xc=xc: E.tensor_copy(
                        out=xc[:, 64:128], in_=vtbb[:, h * 64:(h + 1) * 64]),
                        deps=(lk6,))
                    xprev = (c1, c2)
                    xcur = xc
                    for lv in range(6, 0, -1):
                        mX = p.op("tensor", lambda E, lv=lv, xcur=xcur:
                                  E.matmul(psX[:, :], alv[lv][:, :], xcur[:, :],
                                           start=True, stop=False),
                                  deps=xprev, inc=False)
                        mX2 = p.op("tensor", lambda E, xcur=xcur:
                                   E.matmul(psX[:, :], c.ident_bf[:, :], xcur[:, :],
                                            start=False, stop=True),
                                   deps=(), inc=True)
                        xn_, xbusy2, xj2 = xrot.get()
                        ex = p.op(eng, lambda E, xn_=xn_, eng=eng: E.activation(
                            out=xn_[:, :], in_=psX[:, :], func=AF.Identity)
                            if eng == "scalar" else E.tensor_copy(
                            out=xn_[:, :], in_=psX[:, :]),
                            deps=(mX2, xbusy2))
                        xrot.set_busy(xj2, ex)
                        xcur = xn_
                        xprev = (ex,)
                    mX = p.op("tensor", lambda E, xcur=xcur:
                              E.matmul(psX[:, :], alv[0][:, :], xcur[:, :],
                                       start=True, stop=True), deps=xprev,
                              inc=True)
                    xf, xbusy2, xj2 = xrot.get()
                    exf = p.op("vector", lambda E, xcur=xcur, xf=xf: E.tensor_tensor(
                        out=xf[:, :], in0=xcur[:, :], in1=psX[:, :],
                        op=ALU.subtract), deps=(mX, xbusy2))
                    xrot.set_busy(xj2, exf)
                    tw = p.op("tensor", lambda E, xf=xf: E.transpose(
                        psT3[:, :], xf[:, 0:64], c.ident_bf[:, :]), deps=(exf,),
                        inc=True)
                    ew = p.op("vector", lambda E, hb=hb: E.tensor_copy(
                        out=wTb[hb:hb + 64, :], in_=psT3[0:64, :]), deps=(tw,))
                    mt = p.op("tensor", lambda E, hb=hb, hc=hc: E.matmul(
                        psO2[:, :], wTb[hb:hb + 64, :], Sbf[hb:hb + 64, hc, :],
                        start=True, stop=True), deps=(ew,), inc=True)
                    eu = p.op("vector", lambda E, xf=xf: E.tensor_tensor(
                        out=upp[:, :], in0=xf[:, 64:128], in1=psO2[:, :],
                        op=ALU.subtract), deps=(mt,))
                    bank, busy, j = arot2.get()
                    mM = p.op("tensor", lambda E, ksl=ksl, qsl=qsl, bank=bank:
                              E.matmul(bank[:, :], ksl, qsl, start=True, stop=True),
                              deps=(lk3, busy), inc=True)
                    eM = p.op("vector", lambda E, bank=bank: E.tensor_tensor(
                        out=mpr[:, :], in0=bank[:, :], in1=m01[:, 2, :],
                        op=ALU.mult), deps=(mM,))
                    arot2.set_busy(j, eM)
                    mo1 = p.op("tensor", lambda E, qsl=qsl, hb=hb, hc=hc: E.matmul(
                        psO2[:, :], qsl, Sbf[hb:hb + 64, hc, :],
                        start=True, stop=False), deps=(eu,), inc=False)
                    mo2 = p.op("tensor", lambda E: E.matmul(
                        psO2[:, :], mpr[:, :], upp[:, :], start=False, stop=True),
                        deps=(eM,), inc=True)
                    eo = p.op(eng, lambda E, cc=cc, h=h, eng=eng: E.activation(
                        out=o_t[:, cc, h * 64:(h + 1) * 64], in_=psO2[:, :],
                        func=AF.Identity) if eng == "scalar" else E.tensor_copy(
                        out=o_t[:, cc, h * 64:(h + 1) * 64], in_=psO2[:, :]),
                        deps=(mo2,))
                    ms = p.op("tensor", lambda E, knb=knb, h=h: E.matmul(
                        psS2[:, :], knb[:, h * 64:(h + 1) * 64], upp[:, :],
                        start=True, stop=True), deps=(lk4, eo), inc=True)
                    es1 = p.op("vector", lambda E, hb=hb, hc=hc: E.tensor_tensor(
                        out=Ssb[hb:hb + 64, hc, :], in0=Ssb[hb:hb + 64, hc, :],
                        in1=psS2[:, :], op=ALU.add), deps=(ms,))
                    lasth = p.op("vector", lambda E, hb=hb, hc=hc: E.tensor_copy(
                        out=Sbf[hb:hb + 64, hc, :], in_=Ssb[hb:hb + 64, hc, :]),
                        deps=(es1,))
                k_rot.set_busy(kcj, lasth)
                kt_rot.set_busy(ktj, lasth)
                q_rot.set_busy(qcj, lasth)
                kn_rot.set_busy(knj, lasth)
                kb_rot.set_busy(kbj, lasth)
                v_rot.set_busy(vtj, lasth)
            p.barrier()
        with contextlib.ExitStack() as st2:
            sb3 = lambda n, s, d: st2.enter_context(nc.sbuf_tensor(n, s, d))
            wo = sb3("l2_wo_sb", [128, NC, D], BF16)
            ocht = [sb3(f"l2_ocht{i}", [128, NC, 128], BF16) for i in range(2)]
            brow2 = sb3("l2_brow2", [1, D], BF16)
            pt1 = st2.enter_context(nc.psum_tensor("l2_pt1b", [128, 128], BF16))
            pt2 = st2.enter_context(nc.psum_tensor("l2_pt2b", [128, 128], BF16))
            ps2 = [st2.enter_context(nc.psum_tensor(f"l2_ps2b{i}", [128, 512], F32))
                   for i in range(2)]
            lwo = p.dma("gpsimd", wo[:, :, :],
                        W["l2_wo"].rearrange("(c p) n -> p c n", p=128))
            lbo = p.dma("gpsimd", brow2[:, :], W["l2_bo_row"][:, :])
            rot = Rot(ps2)
            orot = Rot(ocht)
            trot = Rot((pt1, pt2))
            for ti in range(NT):
                ob, obusy, oj = orot.get()
                evs = []
                for ch in range(NC):
                    bank, busy, j = trot.get()
                    tt = p.op("tensor", lambda E, ti=ti, ch=ch, bank=bank:
                              E.transpose(bank[:, :],
                                          o_t[:, ti, ch * 128:(ch + 1) * 128],
                                          c.ident_bf[:, :]),
                              deps=(busy, obusy), inc=True)
                    ev = p.op("vector", lambda E, ch=ch, bank=bank, ob=ob:
                              E.tensor_copy(out=ob[:, ch, :], in_=bank[:, :]),
                              deps=(tt,))
                    trot.set_busy(j, ev)
                    evs.append(ev)
                last = None
                for nh in range(2):
                    bank, busy, j = rot.get()
                    pairs = [(ob[:, ch, :], wo[:, ch, nh * 512:(nh + 1) * 512])
                             for ch in range(NC)]
                    pairs.append((c.ones_row_bf[:, :],
                                  brow2[0:1, nh * 512:(nh + 1) * 512]))
                    mm = mm_group(p, bank[:, :], pairs,
                                  deps=(lwo, lbo, busy) + tuple(evs))
                    last = p.op("vector", lambda E, ti=ti, nh=nh, bank=bank,
                                xnt=c.xn_t:
                                E.tensor_tensor(out=x2_t[:, ti, nh * 512:(nh + 1) * 512],
                                                in0=bank[:, :],
                                                in1=xnt[:, ti, nh * 512:(nh + 1) * 512],
                                                op=ALU.add), deps=(mm,))
                    rot.set_busy(j, last)
                orot.set_busy(oj, last)
            p.barrier()
    with contextlib.ExitStack() as stB:
        sb2 = lambda n, s, d: stB.enter_context(nc.sbuf_tensor(n, s, d))
        h2ch = sb2("l2_h2ch", [128, NC, T], BF16)
        with contextlib.ExitStack() as st2:
            sb3 = lambda n, s, d: st2.enter_context(nc.sbuf_tensor(n, s, d))
            lnb2 = sb3("l2_lnb2", [128, D], F32)
            grow = sb3("l2_grow2", [1, D], F32)
            h1t = sb3("l2_h2t", [128, 1024], BF16)
            rr2 = sb3("l2_rr2", [128, NT], F32)
            pt1 = st2.enter_context(nc.psum_tensor("l2_pt1c", [128, 128], BF16))
            pt2 = st2.enter_context(nc.psum_tensor("l2_pt2c", [128, 128], BF16))
            ps2 = [st2.enter_context(nc.psum_tensor(f"l2_ps2c{i}", [128, 512], F32))
                   for i in range(2)]
            lg_ = p.dma("sync", grow[0:1, :], W["l2_ln2_row"][:, :])
            for nh in range(2):
                bc = mm_group(p, ps2[nh][:, :],
                              [(c.ones_row_f[:, :], grow[0:1, nh * 512:(nh + 1) * 512])],
                              deps=(lg_,))
                p.op("vector", lambda E, nh=nh, bank=ps2[nh]:
                     E.tensor_copy(out=lnb2[:, nh * 512:(nh + 1) * 512],
                                   in_=bank[:, :]), deps=(bc,))
            p.barrier()
            tr = Rot((pt1, pt2))
            prevt = None
            for ti in range(NT):
                q1 = p.op("vector", lambda E, ti=ti: E.scalar_tensor_tensor(
                    out=c.sc_f[:, 0:1024], in0=x2_t[:, ti, :], scalar=1.0,
                    in1=x2_t[:, ti, :], op0=ALU.mult, op1=ALU.mult,
                    accum_out=rr2[:, ti:ti + 1]), deps=(prevt,))
                q2 = p.op("vector", lambda E, ti=ti: E.tensor_scalar_mul(
                    out=rr2[:, ti:ti + 1], in0=rr2[:, ti:ti + 1],
                    scalar1=1.0 / 1024.0), deps=(q1,))
                q3 = p.op("scalar", lambda E, ti=ti: E.activation(
                    out=rr2[:, ti:ti + 1], in_=rr2[:, ti:ti + 1], func=AF.Sqrt,
                    bias=c.cst[:, 1:2]), deps=(q2,))
                q4 = p.op("vector", lambda E, ti=ti: E.reciprocal(
                    out=rr2[:, ti:ti + 1], in_=rr2[:, ti:ti + 1]), deps=(q3,))
                t1 = p.op("vector", lambda E, ti=ti, h1t=h1t: E.scalar_tensor_tensor(
                    out=h1t[:, :], in0=x2_t[:, ti, :], scalar=rr2[:, ti:ti + 1],
                    in1=lnb2[:, :], op0=ALU.mult, op1=ALU.mult), deps=(q4,))
                for ch in range(NC):
                    bank, busy, j = tr.get()
                    tt = p.op("tensor", lambda E, ch=ch, bank=bank, h1t=h1t:
                              E.transpose(
                        bank[:, :], h1t[:, ch * 128:(ch + 1) * 128],
                        c.ident_bf[:, :]), deps=(busy, t1), inc=True)
                    prevt = p.op("vector", lambda E, ti=ti, ch=ch, bank=bank:
                                 E.tensor_copy(out=h2ch[:, ch, ti * 128:(ti + 1) * 128],
                                               in_=bank[:, :]), deps=(tt,))
                    tr.set_busy(j, prevt)
            p.barrier()
        with contextlib.ExitStack() as st2:
            sb3 = lambda n, s, d: st2.enter_context(nc.sbuf_tensor(n, s, d))
            hwin = sb3("l2_hwin", [128, 32, 512], BF16)
            b1c = sb3("l2_b1c", [128, 32], F32)
            brow2 = sb3("l2_brow3", [1, D], BF16)
            w1s = [sb3(f"l2_w1s{i}", [128, NC, 128], BF16) for i in range(2)]
            w2s = [sb3(f"l2_w2s{i}", [128, 1024], BF16) for i in range(2)]
            stl = [sb3(f"l2_stl{i}", [128, D], BF16) for i in range(2)]
            ps8 = [st2.enter_context(nc.psum_tensor(f"l2_ps8_{i}", [128, 512], F32))
                   for i in range(8)]
            psA = ps8[0:2]
            w1src = W["l2_w1"].rearrange("(c p) n -> p c n", p=128)
            w2src = W["l2_w2"].rearrange("(c p) n -> p c n", p=128)
            lb1 = p.dma("sync", b1c[:, :], W["l2_b1_col"][:, :])
            lb2 = p.dma("gpsimd", brow2[:, :], W["l2_b2_row"][:, :])
            srot = Rot(stl)
            for w4 in range(4):
                wrot = Rot(w1s)
                slots = {}
                for oc in range(2):
                    buf, busy, j = wrot.get()
                    slots[oc] = (buf, p.dma("gpsimd", buf[:, :, :],
                                            w1src[:, :, oc * 128:(oc + 1) * 128],
                                            deps=(busy,)), j)
                prot = Rot(psA)
                for oc in range(32):
                    buf, lt, slot = slots.pop(oc)
                    bank, busy, j = prot.get()
                    mm = mm_group(p, bank[:, :],
                                  [(buf[:, kc, :],
                                    h2ch[:, kc, w4 * 512:(w4 + 1) * 512])
                                   for kc in range(NC)], deps=(lt, lb1, busy))
                    ev = p.op("scalar", lambda E, oc=oc, bank=bank: E.activation(
                        out=hwin[:, oc, :], in_=bank[:, :],
                        func=AF.Gelu_apprx_tanh, bias=b1c[:, oc:oc + 1]),
                        deps=(mm,))
                    prot.set_busy(j, ev)
                    if oc + 2 < 32:
                        buf2, busy2, j2 = wrot.get()
                        slots[oc + 2] = (buf2, p.dma(
                            "gpsimd", buf2[:, :, :],
                            w1src[:, :, (oc + 2) * 128:(oc + 3) * 128],
                            deps=(busy2, ev)), j2)
                p.barrier()
                wrot2 = Rot(w2s)
                slots = {}
                for i in range(2):
                    buf, busy, j = wrot2.get()
                    slots[i] = (buf, p.dma("gpsimd", buf[:, :],
                                           w2src[:, i, :], deps=(busy,)), j)
                tok_i = None
                for i in range(32):
                    buf, lt, slot = slots.pop(i)
                    for tc in range(4):
                        for dh in range(2):
                            bank = ps8[tc * 2 + dh]
                            tok_i = p.op(
                                "tensor",
                                lambda E, buf=buf, i=i, tc=tc, dh=dh, bank=bank,
                                s=(i == 0), e=(i == 31):
                                E.matmul(bank[:, :],
                                         hwin[:, i, tc * 128:(tc + 1) * 128],
                                         buf[:, dh * 512:(dh + 1) * 512],
                                         start=s, stop=e, skip_group_check=True),
                                deps=(lt,) if (tc == 0 and dh == 0) else (),
                                inc=(tc == 3 and dh == 1))
                    if i + 2 < 32:
                        buf2, busy2, j2 = wrot2.get()
                        slots[i + 2] = (buf2, p.dma("gpsimd", buf2[:, :],
                                                    w2src[:, i + 2, :],
                                                    deps=(busy2, tok_i)), j2)
                for tc in range(4):
                    ti = w4 * 4 + tc
                    sb_, sbusy, sj = srot.get()
                    evs = []
                    for dh in range(2):
                        bank = ps8[tc * 2 + dh]
                        mb = p.op("tensor", lambda E, bank=bank, dh=dh,
                                  brow2=brow2: E.matmul(
                            bank[:, :], c.ones_row_bf[:, :],
                            brow2[0:1, dh * 512:(dh + 1) * 512],
                            start=False, stop=True, skip_group_check=True),
                            deps=(tok_i, lb2), inc=True)
                        evs.append(p.op("vector",
                                        lambda E, ti=ti, dh=dh, bank=bank, sb_=sb_:
                                        E.tensor_tensor(
                                            out=sb_[:, dh * 512:(dh + 1) * 512],
                                            in0=bank[:, :],
                                            in1=x2_t[:, ti, dh * 512:(dh + 1) * 512],
                                            op=ALU.add), deps=(mb, sbusy)))
                    sd = p.dma("sync", s_dram[ti], sb_[:, :], deps=tuple(evs))
                    srot.set_busy(sj, sd)
                p.barrier()

    stTop.close()
    emit_l2ffn(p, c, W, stack, s_dram)


def emit_l2ffn(p, c, W, stack, s_dram):
    nc = p.nc
    sb = lambda name, shape, dt: stack.enter_context(nc.sbuf_tensor(name, shape, dt))
    HD = 64
    vt_dram = nc.dram_tensor("l2f_vt_scr", [NT, 128, 512], BF16)
    xl = sb("l2f_xl", [128, NT, D], BF16)
    with contextlib.ExitStack() as st2:
        c.xn_t = st2.enter_context(nc.sbuf_tensor("l2f_xn_t", [128, NT, D], BF16))
        emit_ln(p, c, dst=xl)
        p.barrier()
    for lyr in range(2):
        with contextlib.ExitStack() as stL:
            sb2 = lambda n, s, d: stL.enter_context(nc.sbuf_tensor(f"{n}_{lyr}", s, d))
            qch = sb2("l2f_qch", [128, 4, T], BF16)
            kch = sb2("l2f_kch", [128, 4, T], BF16)
            with contextlib.ExitStack() as stP:
                sb3 = lambda n, s, d: stP.enter_context(nc.sbuf_tensor(f"{n}_{lyr}", s, d))
                xch = sb3("l2f_xch", [128, NC, TPAD], BF16)
                rope = sb3("l2f_rope", [128, NT, 256], BF16)
                wq3s = sb3("l2f_wq3s", [128, NC, 512], BF16)
                qs_b = sb3("l2f_qsb", [128, 512], F32)
                ks_b = sb3("l2f_ksb", [128, 512], F32)
                qraw = sb3("l2f_qraw", [128, 512], F32)
                qrot_t = sb3("l2f_qrot", [128, 512], BF16)
                rcol = sb3("l2f_rcol", [128, 8], F32)
                vtile = [sb3(f"l2f_vtl{i}", [128, 512], BF16) for i in range(2)]
                pt1 = stP.enter_context(nc.psum_tensor(f"l2f_pt1_{lyr}", [128, 128], BF16))
                pt2 = stP.enter_context(nc.psum_tensor(f"l2f_pt2_{lyr}", [128, 128], BF16))
                psq = [stP.enter_context(
                    nc.psum_tensor(f"l2f_psq{i}_{lyr}", [128, 512], F32))
                    for i in range(2)]
                lrope = p.dma("gpsimd", rope[:, :, :],
                              W["c_rope"].rearrange("(n p) d -> p n d", p=128))
                gz = p.op("gpsimd", lambda E, xch=xch: E.memset(xch[:, :, 0:PAD], 0.0))
                tr = Rot((pt1, pt2))
                prevt = None
                for ti in range(NT):
                    for ch in range(NC):
                        bank, busy, j = tr.get()
                        tt = p.op("tensor", lambda E, ti=ti, ch=ch, bank=bank:
                                  E.transpose(bank[:, :],
                                              xl[:, ti, ch * 128:(ch + 1) * 128],
                                              c.ident_bf[:, :]),
                                  deps=(busy,), inc=True)
                        prevt = p.op("vector", lambda E, xch=xch, ti=ti, ch=ch, bank=bank:
                                     E.tensor_copy(
                                         out=xch[:, ch,
                                                 PAD + ti * 128:PAD + (ti + 1) * 128],
                                         in_=bank[:, :]), deps=(tt, gz))
                        tr.set_busy(j, prevt)
                p.barrier()
                lqs = p.dma("sync", c.sc_f2[0:1, :],
                            W["l2f_qs_row"][lyr:lyr + 1, :])
                b1 = mm_group(p, psq[0][:, :],
                              [(c.ones_row_f[:, :], c.sc_f2[0:1, :])], deps=(lqs,))
                e1 = p.op("vector", lambda E, qs_b=qs_b, pq=psq[0]: E.tensor_copy(
                    out=qs_b[:, :], in_=pq[:, :]), deps=(b1,))
                lks = p.dma("sync", c.sc_f2[0:1, :],
                            W["l2f_ks_row"][lyr:lyr + 1, :], deps=(e1,))
                b2 = mm_group(p, psq[0][:, :],
                              [(c.ones_row_f[:, :], c.sc_f2[0:1, :])],
                              deps=(lks, e1))
                e2 = p.op("vector", lambda E, ks_b=ks_b, pq=psq[0]: E.tensor_copy(
                    out=ks_b[:, :], in_=pq[:, :]), deps=(b2,))
                p.barrier()
                rot = Rot(psq)
                trr = Rot((pt1, pt2))
                vrot = Rot(vtile)
                lastmm = None
                for which in range(3):
                    lw = p.dma("gpsimd", wq3s[:, :, :],
                               W["l2f_wqkv"][lyr].rearrange("(c p) n -> p c n",
                                                            p=128)
                               [:, :, which * 512:(which + 1) * 512],
                               deps=(lastmm,) if lastmm else ())
                    for ti in range(NT):
                        def lhs(kc, ti=ti, xch=xch):
                            off = PAD + ti * 128 - (1 if kc >= 4 else 0)
                            return xch[:, kc, off:off + 128]
                        bank, busy, j = rot.get()
                        mm = mm_group(p, bank[:, :],
                                      [(lhs(kc), wq3s[:, kc, :])
                                       for kc in range(NC)], deps=(lw, busy))
                        lastmm = mm
                        if which == 2:
                            vb, vbusy, vj = vrot.get()
                            ev = p.op("vector", lambda E, bank=bank, vb=vb:
                                      E.tensor_copy(out=vb[:, :], in_=bank[:, :]),
                                      deps=(mm, vbusy))
                            rot.set_busy(j, ev)
                            dv = p.dma("sync", vt_dram[ti], vb[:, :], deps=(ev,))
                            vrot.set_busy(vj, dv)
                            continue
                        ev = p.op("vector", lambda E, qraw=qraw, bank=bank: E.tensor_copy(
                            out=qraw[:, :], in_=bank[:, :]), deps=(mm,))
                        rot.set_busy(j, ev)
                        sq = p.op("scalar", lambda E, qraw=qraw: E.activation(
                            out=c.sc_f[:, 0:512], in_=qraw[:, :], func=AF.Square),
                            deps=(ev,))
                        rd = p.op("vector", lambda E, rcol=rcol: E.tensor_reduce(
                            out=rcol[:, 0:8],
                            in_=c.sc_f[:, 0:512].rearrange("p (g n) -> p g n", g=8),
                            axis=AX, op=ALU.add), deps=(sq,))
                        s2 = p.op("scalar", lambda E, rcol=rcol: E.activation(
                            out=rcol[:, 0:8], in_=rcol[:, 0:8], func=AF.Sqrt),
                            deps=(rd,))
                        s3 = p.op("vector", lambda E, rcol=rcol: E.tensor_scalar_max(
                            out=rcol[:, 0:8], in0=rcol[:, 0:8], scalar1=1e-12),
                            deps=(s2,))
                        s4 = p.op("vector", lambda E, rcol=rcol: E.reciprocal(
                            out=rcol[:, 0:8], in_=rcol[:, 0:8]))
                        nq = p.op("vector", lambda E, qraw=qraw, rcol=rcol: E.tensor_tensor(
                            out=qraw[:, :].rearrange("p (g n) -> p g n", g=8),
                            in0=qraw[:, :].rearrange("p (g n) -> p g n", g=8),
                            in1=rcol[:, 0:8].rearrange("p (g o) -> p g o", o=1)
                            .broadcast_to([128, 8, 64]), op=ALU.mult))
                        sc_b = qs_b if which == 0 else ks_b
                        nq2 = p.op("vector", lambda E, qraw=qraw, sc_b=sc_b: E.tensor_tensor(
                            out=qraw[:, :], in0=qraw[:, :], in1=sc_b[:, :],
                            op=ALU.mult), deps=(nq,))
                        xs_off = 128 if which == 0 else 192
                        cs = p.op("vector", lambda E, rope=rope, ti=ti, xs_off=xs_off:
                                  E.tensor_tensor(out=c.sc_f[:, 0:64],
                                                  in0=rope[:, ti, 0:64],
                                                  in1=rope[:, ti,
                                                           xs_off:xs_off + 64],
                                                  op=ALU.mult), deps=(nq2, lrope))
                        sn = p.op("vector", lambda E, rope=rope, ti=ti, xs_off=xs_off:
                                  E.tensor_tensor(out=c.sc_f[:, 64:128],
                                                  in0=rope[:, ti, 64:128],
                                                  in1=rope[:, ti,
                                                           xs_off:xs_off + 64],
                                                  op=ALU.mult), deps=(cs,))
                        t1 = p.op("vector", lambda E, qraw=qraw: E.tensor_tensor(
                            out=c.sc_f[:, 128:640],
                            in0=qraw[:, :].rearrange("p (g n) -> p g n", g=8),
                            in1=c.sc_f[:, 0:64].rearrange("p (o n) -> p o n", o=1)
                            .broadcast_to([128, 8, 64]), op=ALU.mult), deps=(sn,))
                        t2 = p.op("vector", lambda E, qraw=qraw: E.tensor_tensor(
                            out=c.sc_f2[:, 0:256],
                            in0=qraw[:, :].rearrange("p (g a n) -> p g a n",
                                                     g=8, a=2)[:, :, 1, :],
                            in1=c.sc_f[:, 64:96].rearrange("p (o n) -> p o n", o=1)
                            .broadcast_to([128, 8, 32]), op=ALU.mult), deps=(t1,))
                        t3 = p.op("vector", lambda E: E.scalar_tensor_tensor(
                            out=c.sc_f[:, 128:640].rearrange(
                                "p (g a n) -> p g a n", g=8, a=2)[:, :, 0, :],
                            in0=c.sc_f2[:, 0:256].rearrange("p (g n) -> p g n",
                                                            g=8),
                            scalar=-1.0,
                            in1=c.sc_f[:, 128:640].rearrange(
                                "p (g a n) -> p g a n", g=8, a=2)[:, :, 0, :],
                            op0=ALU.mult, op1=ALU.add), deps=(t2,))
                        t4 = p.op("vector", lambda E, qraw=qraw: E.tensor_tensor(
                            out=c.sc_f2[:, 0:256],
                            in0=qraw[:, :].rearrange("p (g a n) -> p g a n",
                                                     g=8, a=2)[:, :, 0, :],
                            in1=c.sc_f[:, 96:128].rearrange("p (o n) -> p o n",
                                                            o=1)
                            .broadcast_to([128, 8, 32]), op=ALU.mult), deps=(t3,))
                        t5 = p.op("vector", lambda E: E.tensor_tensor(
                            out=c.sc_f[:, 128:640].rearrange(
                                "p (g a n) -> p g a n", g=8, a=2)[:, :, 1, :],
                            in0=c.sc_f[:, 128:640].rearrange(
                                "p (g a n) -> p g a n", g=8, a=2)[:, :, 1, :],
                            in1=c.sc_f2[:, 0:256].rearrange("p (g n) -> p g n",
                                                            g=8),
                            op=ALU.add), deps=(t4,))
                        scl = 8.0 if which == 0 else 1.0
                        t6 = p.op("vector", lambda E, qrot_t=qrot_t, scl=scl: E.tensor_scalar_mul(
                            out=qrot_t[:, :], in0=c.sc_f[:, 128:640], scalar1=scl),
                            deps=(t5,))
                        dstch = qch if which == 0 else kch
                        lastc = t6
                        for ch4 in range(4):
                            bank2, busy2, j2 = trr.get()
                            tt = p.op("tensor", lambda E, qrot_t=qrot_t, ch4=ch4, bank=bank2:
                                      E.transpose(bank[:, :],
                                                  qrot_t[:, ch4 * 128:
                                                         (ch4 + 1) * 128],
                                                  c.ident_bf[:, :]),
                                      deps=(busy2, lastc), inc=True)
                            lastc = p.op("vector",
                                         lambda E, ti=ti, ch4=ch4, bank=bank2,
                                         dstch=dstch:
                                         E.tensor_copy(
                                             out=dstch[:, ch4,
                                                       ti * 128:(ti + 1) * 128],
                                             in_=bank[:, :]),
                                         deps=(tt,))
                            trr.set_busy(j2, lastc)
                        lastmm = lastc
                p.barrier()
            with contextlib.ExitStack() as stA2:
                sb3 = lambda n, s, d: stA2.enter_context(nc.sbuf_tensor(f"{n}_{lyr}", s, d))
                vring = [sb3(f"l2f_vr{i}", [128, 512], BF16) for i in range(6)]
                wo4 = sb3("l2f_wo4", [128, 4, D], BF16)
                ptile = sb3("l2f_ptile", [128, 640], BF16)
                ptT = sb3("l2f_ptT", [128, 128], BF16)
                ott = sb3("l2f_ott", [128, 512], BF16)
                otT = sb3("l2f_otT", [128, 4, 128], BF16)
                rcolA = sb3("l2f_rcolA", [128, 8], F32)
                zcol = sb3("l2f_zcol", [128, 8], F32)
                pt1 = stA2.enter_context(nc.psum_tensor(f"l2f_pt1a_{lyr}", [128, 128],
                                                        BF16))
                pt2 = stA2.enter_context(nc.psum_tensor(f"l2f_pt2a_{lyr}", [128, 128],
                                                        BF16))
                psq = [stA2.enter_context(
                    nc.psum_tensor(f"l2f_psqa{i}_{lyr}", [128, 512], F32))
                    for i in range(2)]
                psPV = stA2.enter_context(nc.psum_tensor(f"l2f_psPV_{lyr}", [128, 64],
                                                         F32))
                lwo = p.dma("gpsimd", wo4[:, :, :],
                            W["l2f_wo"][lyr].rearrange("(c p) n -> p c n", p=128))
                vld = {}
                vld[0] = p.dma("sync", vring[0][:, :], vt_dram[0])
                vld[1] = p.dma("sync", vring[1][:, :], vt_dram[1])
                orot = Rot((pt1, pt2))
                prevo = None
                for itg in range(NT):
                    j0 = itg - 4
                    jts = [jj for jj in range(j0, itg + 1) if jj >= 0]
                    nj = len(jts)
                    for h in range(8):
                        hb = (h % 2) * 64
                        hc = h // 2
                        qsl = qch[hb:hb + 64, hc, itg * 128:(itg + 1) * 128]
                        mm1 = None
                        for bi, jt in enumerate(jts):
                            bank = psq[bi // 4]
                            col = (bi % 4) * 128
                            mm1 = p.op("tensor",
                                       lambda E, kch=kch, jt=jt, bank=bank, col=col,
                                       qsl=qsl, hb=hb, hc=hc:
                                       E.matmul(bank[:, col:col + 128], qsl,
                                                kch[hb:hb + 64, hc,
                                                    jt * 128:(jt + 1) * 128],
                                                start=True, stop=True,
                                                skip_group_check=True),
                                       deps=(prevo,), inc=True)
                        ma = None
                        if nj == 5:
                            ma = p.op("vector", lambda E, pq=psq[0]:
                                      E.tensor_tensor(
                                out=pq[:, 0:128], in0=pq[:, 0:128],
                                in1=c.mask4[:, 0, 0:128], op=ALU.add),
                                deps=(mm1,))
                        mb = p.op("vector", lambda E, nj=nj,
                                  pq=psq[(nj - 1) // 4]: E.tensor_tensor(
                            out=pq[:, ((nj - 1) % 4) * 128:
                                   ((nj - 1) % 4) * 128 + 128],
                            in0=pq[:, ((nj - 1) % 4) * 128:
                                   ((nj - 1) % 4) * 128 + 128],
                            in1=c.mask_tril_add[:, :], op=ALU.add),
                            deps=(mm1, ma))
                        r1 = p.op("vector", lambda E, rcolA=rcolA, nj=nj, pq=psq[0]:
                                  E.tensor_reduce(
                            out=rcolA[:, 0:1], in_=pq[:, 0:min(nj, 4) * 128],
                            axis=AX, op=ALU.max), deps=(mb,))
                        if nj > 4:
                            r1b = p.op("vector", lambda E, rcolA=rcolA, pq=psq[1]:
                                       E.tensor_reduce(
                                out=rcolA[:, 1:2], in_=pq[:, 0:128],
                                axis=AX, op=ALU.max), deps=(mb,))
                            r1 = p.op("vector", lambda E, rcolA=rcolA: E.tensor_tensor(
                                out=rcolA[:, 0:1], in0=rcolA[:, 0:1],
                                in1=rcolA[:, 1:2], op=ALU.max), deps=(r1, r1b))
                        ng = p.op("vector", lambda E, rcolA=rcolA: E.tensor_scalar_mul(
                            out=rcolA[:, 2:3], in0=rcolA[:, 0:1], scalar1=-1.0),
                            deps=(r1,))
                        ex1 = p.op("scalar", lambda E, ptile=ptile, rcolA=rcolA, nj=nj, pq=psq[0]:
                                   E.activation(
                            out=ptile[:, 0:min(nj, 4) * 128],
                            in_=pq[:, 0:min(nj, 4) * 128], func=AF.Exp,
                            bias=rcolA[:, 2:3], accum_out=rcolA[:, 3:4]),
                            deps=(ng,))
                        if nj > 4:
                            ex2 = p.op("scalar", lambda E, ptile=ptile, rcolA=rcolA, pq=psq[1]:
                                       E.activation(
                                out=ptile[:, 512:640], in_=pq[:, 0:128],
                                func=AF.Exp, bias=rcolA[:, 2:3],
                                accum_out=rcolA[:, 4:5]), deps=(ex1,))
                            zs = p.op("vector", lambda E, rcolA=rcolA: E.tensor_tensor(
                                out=rcolA[:, 3:4], in0=rcolA[:, 3:4],
                                in1=rcolA[:, 4:5], op=ALU.add), deps=(ex2,))
                        else:
                            zs = ex1
                        rz = p.op("vector", lambda E, rcolA=rcolA, zcol=zcol, h=h: E.reciprocal(
                            out=zcol[:, h:h + 1], in_=rcolA[:, 3:4]), deps=(zs,))
                        lastpv = None
                        for bi, jt in enumerate(jts):
                            bank, busy, j = orot.get()
                            tt = p.op("tensor", lambda E, ptile=ptile, bi=bi, bank=bank:
                                      E.transpose(bank[:, :],
                                                  ptile[:, bi * 128:
                                                        (bi + 1) * 128],
                                                  c.ident_bf[:, :]),
                                      deps=(busy, rz), inc=True)
                            pe = p.op("vector", lambda E, ptT=ptT, bank=bank:
                                      E.tensor_copy(out=ptT[:, :],
                                                    in_=bank[:, :]),
                                      deps=(tt,) + ((lastpv,)
                                                    if lastpv else ()))
                            orot.set_busy(j, pe)
                            lastpv = p.op("tensor",
                                          lambda E, psPV=psPV, ptT=ptT, vring=vring, jt=jt, h=h, s=(bi == 0),
                                          e=(bi == nj - 1):
                                          E.matmul(psPV[:, :], ptT[:, :],
                                                   vring[jt % 6]
                                                   [:, h * 64:(h + 1) * 64],
                                                   start=s, stop=e,
                                                   skip_group_check=True),
                                          deps=(pe, vld[jt]), inc=True)
                        prevo = p.op("vector", lambda E, ott=ott, psPV=psPV, zcol=zcol, h=h:
                                     E.tensor_scalar_mul(
                                         out=ott[:, h * 64:(h + 1) * 64],
                                         in0=psPV[:, :],
                                         scalar1=zcol[:, h:h + 1]),
                                     deps=(lastpv,))
                    tevs = []
                    for ch4 in range(4):
                        bank, busy, j = orot.get()
                        tt = p.op("tensor", lambda E, ott=ott, ch4=ch4, bank=bank:
                                  E.transpose(bank[:, :],
                                              ott[:, ch4 * 128:(ch4 + 1) * 128],
                                              c.ident_bf[:, :]),
                                  deps=(busy, prevo), inc=True)
                        te = p.op("vector", lambda E, otT=otT, ch4=ch4, bank=bank:
                                  E.tensor_copy(out=otT[:, ch4, :],
                                                in_=bank[:, :]), deps=(tt,))
                        orot.set_busy(j, te)
                        tevs.append(te)
                    for nh in range(2):
                        mmo = mm_group(p, psq[nh][:, :],
                                       [(otT[:, ch4, :],
                                         wo4[:, ch4, nh * 512:(nh + 1) * 512])
                                        for ch4 in range(4)],
                                       deps=(lwo,) + tuple(tevs))
                        prevo = p.op("vector", lambda E, itg=itg, nnh=nh,
                                     pq=psq[nh]:
                                     E.tensor_tensor(
                                         out=xl[:, itg,
                                                nnh * 512:(nnh + 1) * 512],
                                         in0=pq[:, :],
                                         in1=xl[:, itg,
                                                nnh * 512:(nnh + 1) * 512],
                                         op=ALU.add), deps=(mmo,))
                    t_next = itg + 2
                    if t_next < NT:
                        vld[t_next] = p.dma("sync", vring[t_next % 6][:, :],
                                            vt_dram[t_next],
                                            deps=(prevo,) if t_next >= 6 else ())
                p.barrier()
        with contextlib.ExitStack() as stF:
            sb2 = lambda n, s, d: stF.enter_context(nc.sbuf_tensor(f"{n}_{lyr}", s, d))
            xwin = sb2("l2f_xwin", [128, NC, PAD + 512], BF16)
            lngb = sb2("l2f_lngb", [128, 16], F32)
            lnbc = sb2("l2f_lnbc", [128, 16], F32)
            stat = sb2("l2f_stat", [1, 1536], F32)
            bcm = sb2("l2f_bcm", [128, 512], F32)
            bcr = sb2("l2f_bcr", [128, 512], F32)
            h2w = sb2("l2f_h2w", [128, NC, 512], BF16)
            abuf = sb2("l2f_abuf", [128, 512], BF16)
            bbuf = sb2("l2f_bbuf", [128, 512], BF16)
            ybuf = sb2("l2f_ybuf", [128, 512], BF16)
            ych = sb2("l2f_ych", [128, 22, 512], BF16)
            w1sl = [sb2(f"l2f_w1sl{i}", [128, NC, 512], BF16) for i in range(2)]
            b1sl = [sb2(f"l2f_b1sl{i}", [1, 512], BF16) for i in range(2)]
            w2sl = [sb2(f"l2f_w2sl{i}", [128, 512], BF16) for i in range(2)]
            b2r = sb2("l2f_b2r", [1, 1024], BF16)
            pt1 = stF.enter_context(nc.psum_tensor(f"l2f_pt1b_{lyr}", [128, 128], BF16))
            pt2 = stF.enter_context(nc.psum_tensor(f"l2f_pt2b_{lyr}", [128, 128], BF16))
            psF = [stF.enter_context(nc.psum_tensor(f"l2f_psF{i}_{lyr}", [128, 512],
                                                    F32))
                   for i in range(4)]
            lgc = p.dma("sync", lngb[:, :], W["l2f_lng_col"][:, :])
            lbc = p.dma("sync", lnbc[:, :], W["l2f_lnb_col"][:, :])
            lb2r = p.dma("gpsimd", b2r[:, :], W["l2f_b2_row"][lyr])
            w1d = W["l2f_w1"][lyr].rearrange("(c p) n -> p c n", p=128)
            b1d = W["l2f_b1_row"][lyr]
            acs = [512] * 5 + [170]
            a0s = [sum(acs[:i]) for i in range(6)]
            chunks = []
            for k in range(6):
                chunks.append((a0s[k], acs[k], False))
                chunks.append((2730 + a0s[k], acs[k], True))
            NCHK = len(chunks)
            prevw = None
            for w4 in range(4):
                if w4 == 0:
                    gz = p.op("gpsimd", lambda E, xwin=xwin: E.memset(xwin[:, :, 0:PAD],
                                                           0.0))
                else:
                    gz = p.op("vector", lambda E, xwin=xwin: E.tensor_copy(
                        out=xwin[:, :, PAD - 1:PAD],
                        in_=xwin[:, :, PAD + 511:PAD + 512]), deps=(prevw,))
                tr = Rot((pt1, pt2))
                prevt = gz
                for tc2 in range(4):
                    ti = w4 * 4 + tc2
                    for ch in range(NC):
                        bank, busy, j = tr.get()
                        tt = p.op("tensor", lambda E, ti=ti, ch=ch, bank=bank:
                                  E.transpose(bank[:, :],
                                              xl[:, ti, ch * 128:(ch + 1) * 128],
                                              c.ident_bf[:, :]),
                                  deps=(busy, gz), inc=True)
                        prevt = p.op("vector",
                                     lambda E, xwin=xwin, tc2=tc2, ch=ch, bank=bank:
                                     E.tensor_copy(
                                         out=xwin[:, ch, PAD + tc2 * 128:
                                                  PAD + (tc2 + 1) * 128],
                                         in_=bank[:, :]), deps=(tt,))
                        tr.set_busy(j, prevt)
                p.barrier()

                def shsl(kc, xwin=xwin):
                    off = PAD - (1 if kc >= 4 else 0)
                    return xwin[:, kc, off:off + 512]
                prevq = None
                for kc in range(NC):
                    prevq = p.op("tensor", lambda E, shsl=shsl, psF=psF, kc=kc, s=(kc == 0),
                                 e=(kc == 7):
                                 E.matmul(psF[2][0:1, :], c.ones_col_bf[:, :],
                                          shsl(kc), start=s, stop=e,
                                          skip_group_check=True),
                                 deps=(prevt,), inc=(kc == 7))
                cpm = p.op("vector", lambda E, psF=psF, stat=stat: E.tensor_copy(
                    out=stat[0:1, 0:512], in_=psF[2][0:1, :]), deps=(prevq,))
                prevsq = cpm
                for kc in range(NC):
                    sq = p.op("scalar", lambda E, shsl=shsl, kc=kc: E.activation(
                        out=c.sc_f[:, 0:512], in_=shsl(kc), func=AF.Square),
                        deps=(prevsq,))
                    prevsq = p.op("tensor", lambda E, psF=psF, kc=kc, s=(kc == 0),
                                  e=(kc == 7):
                                  E.matmul(psF[3][0:1, :], c.ones_col_f[:, :],
                                           c.sc_f[:, 0:512], start=s, stop=e,
                                           skip_group_check=True),
                                  deps=(sq,), inc=True)
                cps = p.op("vector", lambda E, psF=psF, stat=stat: E.tensor_copy(
                    out=stat[0:1, 512:1024], in_=psF[3][0:1, :]), deps=(prevsq,))
                s1 = p.op("vector", lambda E, stat=stat: E.tensor_scalar_mul(
                    out=stat[0:1, 0:512], in0=stat[0:1, 0:512],
                    scalar1=1.0 / 1024.0), deps=(cpm, cps))
                s2 = p.op("vector", lambda E, stat=stat: E.tensor_tensor(
                    out=stat[0:1, 1024:1536], in0=stat[0:1, 0:512],
                    in1=stat[0:1, 0:512], op=ALU.mult))
                s3 = p.op("vector", lambda E, stat=stat: E.tensor_scalar_mul(
                    out=stat[0:1, 512:1024], in0=stat[0:1, 512:1024],
                    scalar1=1.0 / 1024.0))
                s4 = p.op("vector", lambda E, stat=stat: E.tensor_tensor(
                    out=stat[0:1, 512:1024], in0=stat[0:1, 512:1024],
                    in1=stat[0:1, 1024:1536], op=ALU.subtract))
                s5 = p.op("scalar", lambda E, stat=stat: E.activation(
                    out=stat[0:1, 512:1024], in_=stat[0:1, 512:1024],
                    func=AF.Sqrt, bias=c.cst[0:1, 0:1]), deps=(s4,))
                s6 = p.op("vector", lambda E, stat=stat: E.reciprocal(
                    out=stat[0:1, 512:1024], in_=stat[0:1, 512:1024]),
                    deps=(s5,))
                bm = mm_group(p, psF[2][:, :],
                              [(c.ones_row_f[:, :], stat[0:1, 0:512])],
                              deps=(s6,))
                em = p.op("vector", lambda E, bcm=bcm, psF=psF: E.tensor_copy(
                    out=bcm[:, :], in_=psF[2][:, :]), deps=(bm,))
                br = mm_group(p, psF[3][:, :],
                              [(c.ones_row_f[:, :], stat[0:1, 512:1024])],
                              deps=(em,))
                er = p.op("vector", lambda E, bcr=bcr, psF=psF: E.tensor_copy(
                    out=bcr[:, :], in_=psF[3][:, :]), deps=(br,))
                prevh = er
                for kc in range(NC):
                    hh1 = p.op("vector", lambda E, shsl=shsl, bcm=bcm, kc=kc: E.tensor_tensor(
                        out=c.sc_f[:, 0:512], in0=shsl(kc), in1=bcm[:, :],
                        op=ALU.subtract), deps=(prevh, lgc, lbc))
                    hh2 = p.op("vector", lambda E, bcr=bcr: E.tensor_tensor(
                        out=c.sc_f[:, 0:512], in0=c.sc_f[:, 0:512],
                        in1=bcr[:, :], op=ALU.mult))
                    prevh = p.op("vector", lambda E, h2w=h2w, lnbc=lnbc, lngb=lngb, lyr=lyr, kc=kc: E.tensor_scalar(
                        out=h2w[:, kc, :], in0=c.sc_f[:, 0:512],
                        scalar1=lngb[:, lyr * 8 + kc:lyr * 8 + kc + 1],
                        scalar2=lnbc[:, lyr * 8 + kc:lyr * 8 + kc + 1],
                        op0=ALU.mult, op1=ALU.add))
                p.barrier()
                trg = Rot((pt1, pt2))
                prevy = None
                wrt = Rot(w1sl)
                brt = Rot(b1sl)
                for tc in range(4):
                    slots = {}
                    for i2 in range(2):
                        n0, nw, _ = chunks[i2]
                        buf, busy, j = wrt.get()
                        bb, bbusy, bj = brt.get()
                        d1 = p.dma("gpsimd", buf[:, :, 0:nw],
                                   w1d[:, :, n0:n0 + nw], deps=(busy,))
                        d2 = p.dma("gpsimd", bb[0:1, 0:nw],
                                   b1d[0:1, n0:n0 + nw], deps=(bbusy,))
                        slots[i2] = (buf, bb, d1, d2, j, bj)
                    prot = Rot(psF[0:2])
                    evA = None
                    for nchk in range(NCHK):
                        buf, bb, lt, lt2, slot, bslot = slots.pop(nchk)
                        n0, nw, is_b = chunks[nchk]
                        bank, busy, j = prot.get()
                        pairs = [(h2w[:, kc, tc * 128:(tc + 1) * 128],
                                  buf[:, kc, 0:nw]) for kc in range(NC)]
                        pairs.append((c.ones_row_bf[:, :], bb[0:1, 0:nw]))
                        mm = mm_group(p, bank[:, 0:nw], pairs,
                                      deps=(lt, lt2, busy, prevy))
                        if is_b:
                            ev = p.op("scalar", lambda E, bbuf=bbuf, nw=nw, bank=bank:
                                      E.activation(out=bbuf[:, 0:nw],
                                                   in_=bank[:, 0:nw],
                                                   func=AF.Gelu), deps=(mm, evA))
                        else:
                            ev = p.op("vector", lambda E, abuf=abuf, nw=nw, bank=bank:
                                      E.tensor_copy(out=abuf[:, 0:nw],
                                                    in_=bank[:, 0:nw]),
                                      deps=(mm,))
                        evA = ev
                        prot.set_busy(j, ev)
                        wrt.set_busy(slot, ev)
                        brt.set_busy(bslot, ev)
                        if nchk + 2 < NCHK:
                            n0n, nwn, _ = chunks[nchk + 2]
                            buf2, busy2, j2 = wrt.get()
                            bb2, bbusy2, bj2 = brt.get()
                            d1 = p.dma("gpsimd", buf2[:, :, 0:nwn],
                                       w1d[:, :, n0n:n0n + nwn], deps=(busy2,))
                            d2 = p.dma("gpsimd", bb2[0:1, 0:nwn],
                                       b1d[0:1, n0n:n0n + nwn], deps=(bbusy2,))
                            slots[nchk + 2] = (buf2, bb2, d1, d2, j2, bj2)
                        if not is_b:
                            continue
                        k = nchk // 2
                        nwk = acs[k]
                        ym = p.op("vector", lambda E, abuf=abuf, bbuf=bbuf, ybuf=ybuf, nwk=nwk: E.tensor_tensor(
                            out=ybuf[:, 0:nwk], in0=abuf[:, 0:nwk],
                            in1=bbuf[:, 0:nwk], op=ALU.mult), deps=(ev,))
                        prevy = ym
                        sws = [128] * (nwk // 128) + ([nwk % 128]
                                                      if nwk % 128 else [])
                        for si, sw in enumerate(sws):
                            ck = k * 4 + si
                            bank2, busy2, j2 = trg.get()
                            tt = p.op("tensor",
                                      lambda E, ybuf=ybuf, si=si, sw=sw, bank=bank2:
                                      E.transpose(bank[0:sw, 0:128],
                                                  ybuf[:, si * 128:
                                                       si * 128 + sw],
                                                  c.ident_bf[:, :]),
                                      deps=(busy2, ym), inc=True)
                            prevy = p.op("vector",
                                         lambda E, ych=ych, ck=ck, sw=sw, tc=tc,
                                         bank=bank2:
                                         E.tensor_copy(
                                             out=ych[0:sw, ck,
                                                     tc * 128:(tc + 1) * 128],
                                             in_=bank[0:sw, 0:128]),
                                         deps=(tt,))
                            trg.set_busy(j2, prevy)
                p.barrier()
                kcs = [128] * 21 + [42]
                lastev = None
                wrt2 = Rot(w2sl)
                for dh in range(2):
                    slots = {}
                    for i2 in range(2):
                        buf, busy, j = wrt2.get()
                        slots[i2] = (buf, p.dma(
                            "gpsimd", buf[0:kcs[i2], :],
                            W["l2f_w2"][lyr][i2 * 128:i2 * 128 + kcs[i2],
                                             dh * 512:(dh + 1) * 512],
                            deps=(busy,)), j)
                    tok_i = None
                    for i in range(22):
                        buf, lt, slot = slots.pop(i)
                        kw = kcs[i]
                        for tc in range(4):
                            bank = psF[tc]
                            tok_i = p.op(
                                "tensor",
                                lambda E, ych=ych, buf=buf, i=i, tc=tc, bank=bank,
                                kw=kw, s=(i == 0):
                                E.matmul(bank[:, :],
                                         ych[0:kw, i, tc * 128:(tc + 1) * 128],
                                         buf[0:kw, :],
                                         start=s, stop=False,
                                         skip_group_check=True),
                                deps=((lt,) if tc == 0 else ())
                                + ((lastev,) if (i == 0 and tc == 0) else ()),
                                inc=(tc == 3))
                        wrt2.set_busy(slot, tok_i)
                        if i + 2 < 22:
                            buf2, busy2, j2 = wrt2.get()
                            slots[i + 2] = (buf2, p.dma(
                                "gpsimd", buf2[0:kcs[i + 2], :],
                                W["l2f_w2"][lyr]
                                [(i + 2) * 128:(i + 2) * 128 + kcs[i + 2],
                                 dh * 512:(dh + 1) * 512],
                                deps=(busy2,)), j2)
                    for tc in range(4):
                        ti = w4 * 4 + tc
                        bank = psF[tc]
                        mb = p.op("tensor", lambda E, b2r=b2r, bank=bank, dh=dh:
                                  E.matmul(bank[:, :], c.ones_row_bf[:, :],
                                           b2r[0:1, dh * 512:(dh + 1) * 512],
                                           start=False, stop=True,
                                           skip_group_check=True),
                                  deps=(tok_i, lb2r), inc=True)
                        lastev = p.op("vector",
                                      lambda E, ti=ti, dh=dh, bank=bank:
                                      E.tensor_tensor(
                                          out=xl[:, ti, dh * 512:(dh + 1) * 512],
                                          in0=bank[:, :],
                                          in1=xl[:, ti,
                                                 dh * 512:(dh + 1) * 512],
                                          op=ALU.add), deps=(mb,))
                prevw = lastev
                p.barrier()
    with contextlib.ExitStack() as st2:
        stl = [st2.enter_context(nc.sbuf_tensor(f"l2f_stl{i}", [128, D], BF16))
               for i in range(2)]
        srot = Rot(stl)
        prev_add = None
        for ti in range(NT):
            sb_, sbusy, sj = srot.get()
            sld = p.dma("sync", sb_[:, :], s_dram[ti], deps=(sbusy,))
            for nh in range(2):
                g1_ = p.op("scalar", lambda E, ti=ti, nh=nh: E.activation(
                    out=c.sc_f[:, 0:512], in_=xl[:, ti, nh * 512:(nh + 1) * 512],
                    func=AF.Silu), deps=(prev_add,))
                g2_ = p.op("vector", lambda E, nh=nh, sb_=sb_: E.tensor_tensor(
                    out=c.sc_f2[:, 0:512], in0=c.sc_f[:, 0:512],
                    in1=sb_[:, nh * 512:(nh + 1) * 512], op=ALU.mult),
                    deps=(g1_, sld))
                prev_add = p.op("vector", lambda E, ti=ti, nh=nh: E.tensor_tensor(
                    out=c.x[:, ti, nh * 512:(nh + 1) * 512],
                    in0=c.x[:, ti, nh * 512:(nh + 1) * 512],
                    in1=c.sc_f2[:, 0:512], op=ALU.add), deps=(g2_,))
            srot.set_busy(sj, prev_add)
        p.barrier()



def emit_layer3(p, c, W, stack, dbg=None):
    nc = p.nc
    sb = lambda name, shape, dt: stack.enter_context(nc.sbuf_tensor(name, shape, dt))

    xn_ch = sb("l3_xn_ch", [128, NC, TPAD], BF16)
    s_buf = sb("l3_s", [128, NT, D], BF16)
    fcol = sb("l3_f", [128, NT], F32)

    with contextlib.ExitStack() as st2:
        c.xn_t = st2.enter_context(nc.sbuf_tensor("l3_xn_t", [128, NT, D], BF16))
        pt1 = st2.enter_context(nc.psum_tensor("l3_pt1", [128, 128], BF16))
        pt2 = st2.enter_context(nc.psum_tensor("l3_pt2", [128, 128], BF16))
        emit_ln(p, c)
        p.barrier()
        emit_t2ch(p, c, c.xn_t, xn_ch, (pt1, pt2))
        p.barrier()

    stA = stack.enter_context(contextlib.ExitStack())
    o_ch = stA.enter_context(nc.sbuf_tensor("l3_o_ch", [128, 8, T], BF16))
    with contextlib.ExitStack() as st2:
        sb2 = lambda name, shape, dt: st2.enter_context(nc.sbuf_tensor(name, shape, dt))
        qch = sb2("l3_qch", [128, T], BF16)
        kch = sb2("l3_kch", [128, T], BF16)
        vpr = sb2("l3_vpr", [128, NT, 128], BF16)
        wsl = [sb2(f"l3_wsl{i}", [128, NC, 128], BF16) for i in range(6)]
        bcol = sb2("l3_bcol", [128, 24], F32)
        brow = sb2("l3_brow", [1, 1024], BF16)
        negbw = sb2("l3_negbw", [1, 512], BF16)
        nrm = sb2("l3_nrm", [1, 16], F32)
        rowt = sb2("l3_rowt", [1, 512], F32)
        rzb = sb2("l3_rzb", [128, 512], F32)
        pj = [sb2(f"l3_pj{i}", [128, 512], BF16) for i in range(2)]
        psS = [st2.enter_context(nc.psum_tensor(f"l3_psS{i}", [128, 512], F32))
               for i in range(2)]
        psZ = st2.enter_context(nc.psum_tensor("l3_psZ", [128, 512], F32))
        psOA = st2.enter_context(nc.psum_tensor("l3_psOA", [128, 512], F32))
        psOB = st2.enter_context(nc.psum_tensor("l3_psOB", [128, 512], F32))
        psN = st2.enter_context(nc.psum_tensor("l3_psN", [128, 512], F32))

        lbc = p.dma("sync", bcol[:, :], W["l3_battn_col"][:, :])
        lbr = p.dma("gpsimd", brow[:, :], W["l3_battn_row"][0:1, 2048:3072])
        wsrc = W["l3_wattn"].rearrange("(c p) n -> p c n", p=128)

        def wld(slot, pair, deps=()):
            t1 = p.dma("gpsimd", wsl[3 * slot][:, :, :],
                       wsrc[:, :, pair * 128:(pair + 1) * 128], deps=deps)
            t2 = p.dma("gpsimd", wsl[3 * slot + 1][:, :, :],
                       wsrc[:, :, 1024 + pair * 128:1024 + (pair + 1) * 128])
            t3 = p.dma("gpsimd", wsl[3 * slot + 2][:, :, :],
                       wsrc[:, :, 2048 + pair * 128:2048 + (pair + 1) * 128])
            return (t1, t2, t3)

        lds = {0: wld(0, 0), 1: wld(1, 1)}
        for pair in range(8):
            slot = pair % 2
            ltq, ltk, ltv = lds.pop(pair)
            rot = Rot(psS)
            for dst, wi, boff, lt in ((qch, 0, pair, ltq), (kch, 1, 8 + pair, ltk)):
                for w4 in range(4):
                    bank, busy, j = rot.get()
                    pairs = [(wsl[3 * slot + wi][:, kc, :],
                              xn_ch[:, kc, PAD + w4 * 512:PAD + (w4 + 1) * 512])
                             for kc in range(NC)]
                    mm = mm_group(p, bank[:, :], pairs, deps=(lt, lbc, busy))
                    ev = p.op("scalar", lambda E, dst=dst, w4=w4, bank=bank, boff=boff:
                              E.activation(out=dst[:, w4 * 512:(w4 + 1) * 512],
                                           in_=bank[:, :], func=AF.Identity,
                                           bias=bcol[:, boff:boff + 1]), deps=(mm,))
                    rot.set_busy(j, ev)
            for ti in range(NT):
                bank, busy, j = rot.get()
                pairs = [(xn_ch[:, kc, PAD + ti * 128:PAD + (ti + 1) * 128],
                          wsl[3 * slot + 2][:, kc, :]) for kc in range(NC)]
                pairs.append((c.ones_row_bf[:, :],
                              brow[0:1, pair * 128:(pair + 1) * 128]))
                mm = mm_group(p, bank[:, 0:128], pairs, deps=(ltv, lbr, busy))
                ev = p.op("vector", lambda E, ti=ti, bank=bank: E.tensor_copy(
                    out=vpr[:, ti, :], in_=bank[:, 0:128]), deps=(mm,))
                rot.set_busy(j, ev)
            if pair + 2 < 8:
                lds[pair + 2] = wld(slot, pair + 2,
                                    deps=(("e", "tensor", p.cnt["tensor"]),))
            p.barrier()

            prev = None
            for head, sel in ((0, c.selA), (1, c.selB)):
                for w4 in range(4):
                    sq = p.op("scalar", lambda E, w4=w4: E.activation(
                        out=c.sc_f[:, 0:512],
                        in_=kch[:, w4 * 512:(w4 + 1) * 512], func=AF.Square),
                        deps=(prev,) if prev else ())
                    mq = mm_group(p, psN[0:1, 0:512],
                                  [(sel[:, :], c.sc_f[:, 0:512])], deps=(sq, prev))
                    prev = p.op("vector", lambda E, w4=w4, head=head: E.tensor_reduce(
                        out=nrm[0:1, 4 * head + w4:4 * head + w4 + 1],
                        in_=psN[0:1, 0:512], axis=AX, op=ALU.max), deps=(mq,))
                prev = p.op("vector", lambda E, head=head: E.tensor_reduce(
                    out=nrm[0:1, 8 + head:9 + head], in_=nrm[0:1, 4 * head:4 * head + 4],
                    axis=AX, op=ALU.max), deps=(prev,))
            p.barrier()

            prev_store = None
            for head, sel_h, psO in ((0, c.selA, psOA), (1, c.selB, psOB)):
                base = head * 64
                hsl = slice(base, base + 64)
                for iw in range(4):
                    njt = iw * 4 + 4
                    sq = p.op("scalar", lambda E, iw=iw, hsl=hsl: E.activation(
                        out=c.sc_f[hsl, 0:512],
                        in_=qch[hsl, iw * 512:(iw + 1) * 512],
                        func=AF.Square), deps=(prev_store,))
                    mq = mm_group(p, psN[0:1, 0:512],
                                  [(sel_h[hsl, :], c.sc_f[hsl, 0:512])], deps=(sq,))
                    r1 = p.op("vector", lambda E, head=head: E.tensor_scalar_mul(
                        out=rowt[0:1, :], in0=psN[0:1, 0:512],
                        scalar1=nrm[0:1, 8 + head:9 + head]), deps=(mq,))
                    r2 = p.op("scalar", lambda E: E.activation(
                        out=rowt[0:1, :], in_=rowt[0:1, :], func=AF.Sqrt), deps=(r1,))
                    r3 = p.op("vector", lambda E: E.tensor_scalar_mul(
                        out=negbw[0:1, :], in0=rowt[0:1, :], scalar1=-1.0), deps=(r2,))
                    prot = Rot(list(zip(pj, psS)))
                    tv = None
                    for jt in range(njt):
                        (pbuf, bank), busy, j = prot.get()
                        pairs2 = [(kch[hsl, jt * 128:(jt + 1) * 128],
                                   qch[hsl, iw * 512:(iw + 1) * 512]),
                                  (c.ones_row_bf[:, :], negbw[0:1, 0:512])]
                        mm = mm_group(p, bank[:, :], pairs2, deps=(busy, r3))
                        dk = jt - iw * 4
                        if dk >= 0:
                            mm = p.op("vector", lambda E, bank=bank, dk=dk:
                                      E.tensor_tensor(out=bank[:, :], in0=bank[:, :],
                                                      in1=c.mask4[:, dk, :],
                                                      op=ALU.add), deps=(mm,))
                        ex = p.op("scalar", lambda E, pbuf=pbuf, bank=bank:
                                  E.activation(out=pbuf[:, :], in_=bank[:, :],
                                               func=AF.Exp, scale=0.125), deps=(mm,))
                        lastj = jt == njt - 1
                        tz = p.op("tensor", lambda E, pbuf=pbuf, s=(jt == 0), e=lastj:
                                  E.matmul(psZ[0:1, :], c.ones_col_bf[:, :],
                                           pbuf[:, :], start=s, stop=e,
                                           skip_group_check=True),
                                  deps=(ex, prev_store), inc=lastj)
                        tv = p.op("tensor",
                                  lambda E, pbuf=pbuf, jt=jt, s=(jt == 0), e=lastj,
                                  psO=psO, head=head:
                                  E.matmul(psO[0:64, :],
                                           vpr[:, jt, head * 64:head * 64 + 64],
                                           pbuf[:, :], start=s, stop=e,
                                           skip_group_check=True),
                                  deps=(), inc=True)
                        prot.set_busy(j, tv)
                    rz = p.op("vector", lambda E: E.reciprocal(
                        out=rowt[0:1, 0:512], in_=psZ[0:1, :]), deps=(tz, tv))
                    bc = mm_group(p, psN[:, :],
                                  [(c.ones_row_f[:, :], rowt[0:1, 0:512])],
                                  deps=(rz,))
                    cb = p.op("vector", lambda E: E.tensor_copy(
                        out=rzb[:, :], in_=psN[:, :]), deps=(bc,))
                    prev_store = p.op(
                        "vector", lambda E, psO=psO, head=head, iw=iw, pair=pair:
                        E.tensor_tensor(
                            out=o_ch[slice(0, 64) if head == 0 else slice(64, 128),
                                     pair, iw * 512:(iw + 1) * 512],
                            in0=psO[0:64, :], in1=rzb[0:64, :], op=ALU.mult),
                        deps=(cb,))
            p.barrier()

    with contextlib.ExitStack() as st2:
        wp = st2.enter_context(nc.sbuf_tensor("l3_wp", [128, NC, D], BF16))
        bpr = st2.enter_context(nc.sbuf_tensor("l3_bpr", [1, D], BF16))
        psP = [st2.enter_context(nc.psum_tensor(f"l3_psP{i}", [128, 512], F32))
               for i in range(2)]
        l1 = p.dma("gpsimd", wp[:, :, :],
                   W["l3_wproj"].rearrange("(c p) n -> p c n", p=128))
        l2 = p.dma("gpsimd", bpr[:, :], W["l3_bproj_row"][:, :])
        rot = Rot(psP)
        for ti in range(NT):
            for nh in range(2):
                bank, busy, j = rot.get()
                pairs = [(o_ch[:, ch, ti * 128:(ti + 1) * 128],
                          wp[:, ch, nh * 512:(nh + 1) * 512]) for ch in range(NC)]
                pairs.append((c.ones_row_bf[:, :], bpr[0:1, nh * 512:(nh + 1) * 512]))
                mm = mm_group(p, bank[:, :], pairs, deps=(l1, l2, busy))
                ev = p.op("vector", lambda E, ti=ti, nh=nh, bank=bank: E.tensor_copy(
                    out=s_buf[:, ti, nh * 512:(nh + 1) * 512], in_=bank[:, :]),
                    deps=(mm,))
                rot.set_busy(j, ev)
        p.barrier()
    stA.close()

    with contextlib.ExitStack() as st2:
        wg = st2.enter_context(nc.sbuf_tensor("l3_wgs", [128, NC, 8], BF16))
        psG = st2.enter_context(nc.psum_tensor("l3_psG", [128, 512], F32))
        lg = p.dma("gpsimd", wg[:, :, :],
                   W["l3_wg"].rearrange("(c p) n -> p c n", p=128))
        sc = c.sc_f
        prev = None
        for ti in range(NT):
            pairs = [(xn_ch[:, kc, PAD + ti * 128:PAD + (ti + 1) * 128],
                      wg[:, kc, :]) for kc in range(NC)]
            mm = mm_group(p, psG[:, 0:8], pairs,
                          deps=(lg,) if prev is None else (lg, prev))
            prev = emit_top2_factor(p, c, psG[:, 0:8], fcol[:, ti:ti + 1], sc, mm)
        p.barrier()

    with contextlib.ExitStack() as st2:
        sb2 = lambda name, shape, dt: st2.enter_context(nc.sbuf_tensor(name, shape, dt))
        h1 = sb2("l3_h1", [128, 32, 512], BF16)
        w1s = [sb2(f"l3_w1s{i}", [128, NC, 128], BF16) for i in range(2)]
        w2s = [sb2(f"l3_w2s{i}", [128, 1024], BF16) for i in range(2)]
        ps8 = [st2.enter_context(nc.psum_tensor(f"l3_ps8_{i}", [128, 512], F32))
               for i in range(8)]
        psA = ps8[0:2]
        w1src = W["l3_w1"].rearrange("(c p) n -> p c n", p=128)
        w2src = W["l3_w2"].rearrange("(c p) n -> p c n", p=128)
        for w4 in range(4):
            wrot = Rot(w1s)
            slots = {}
            for oc in range(2):
                buf, busy, j = wrot.get()
                slots[oc] = (buf, p.dma("gpsimd", buf[:, :, :],
                                        w1src[:, :, oc * 128:(oc + 1) * 128],
                                        deps=(busy,)), j)
            prot = Rot(psA)
            for oc in range(32):
                buf, lt, slot = slots.pop(oc)
                bank, busy, j = prot.get()
                pairs = [(buf[:, kc, :],
                          xn_ch[:, kc, PAD + w4 * 512:PAD + (w4 + 1) * 512])
                         for kc in range(NC)]
                mm = mm_group(p, bank[:, :], pairs, deps=(lt, busy))
                ev = p.op("scalar", lambda E, oc=oc, bank=bank: E.activation(
                    out=h1[:, oc, :], in_=bank[:, :], func=AF.Relu), deps=(mm,))
                prot.set_busy(j, ev)
                if oc + 2 < 32:
                    buf2, busy2, j2 = wrot.get()
                    slots[oc + 2] = (buf2, p.dma("gpsimd", buf2[:, :, :],
                                                 w1src[:, :, (oc + 2) * 128:(oc + 3) * 128],
                                                 deps=(busy2, ev)), j2)
            p.barrier()
            if True:
                wrot2 = Rot(w2s)
                slots = {}
                for i in range(2):
                    buf, busy, j = wrot2.get()
                    slots[i] = (buf, p.dma("gpsimd", buf[:, :],
                                           w2src[:, i, :], deps=(busy,)), j)
                mm_last = {}
                for i in range(32):
                    buf, lt, slot = slots.pop(i)
                    tok_i = None
                    for tc in range(4):
                        for dh in range(2):
                            bank = ps8[tc * 2 + dh]
                            tok_i = p.op(
                                "tensor",
                                lambda E, buf=buf, i=i, tc=tc, dh=dh, bank=bank,
                                s=(i == 0), e=(i == 31):
                                E.matmul(bank[:, :],
                                         h1[:, i, tc * 128:(tc + 1) * 128],
                                         buf[:, dh * 512:(dh + 1) * 512],
                                         start=s, stop=e, skip_group_check=True),
                                deps=(lt,) if (tc == 0 and dh == 0) else (),
                                inc=(tc == 3 and dh == 1))
                    if i == 31:
                        for b8 in range(8):
                            mm_last[b8] = tok_i
                    if i + 2 < 32:
                        buf2, busy2, j2 = wrot2.get()
                        slots[i + 2] = (buf2, p.dma("gpsimd", buf2[:, :],
                                                    w2src[:, i + 2, :],
                                                    deps=(tok_i,)), j2)
                prev_add = None
                for tc in range(4):
                    ti = w4 * 4 + tc
                    for dh in range(2):
                        bank = ps8[tc * 2 + dh]
                        g1 = p.op("scalar", lambda E, bank=bank, ti=ti: E.activation(
                            out=c.sc_f[:, 0:512], in_=bank[:, :], func=AF.Silu,
                            scale=fcol[:, ti:ti + 1]),
                            deps=(mm_last[tc * 2 + dh], prev_add))
                        g2 = p.op("vector", lambda E, ti=ti, dh=dh: E.tensor_tensor(
                            out=c.sc_f2[:, 0:512], in0=c.sc_f[:, 0:512],
                            in1=s_buf[:, ti, dh * 512:(dh + 1) * 512],
                            op=ALU.mult), deps=(g1,))
                        prev_add = p.op("vector", lambda E, ti=ti, dh=dh: E.tensor_tensor(
                            out=c.x[:, ti, dh * 512:(dh + 1) * 512],
                            in0=c.x[:, ti, dh * 512:(dh + 1) * 512],
                            in1=c.sc_f2[:, 0:512], op=ALU.add), deps=(g2,))
                p.barrier()
    p.barrier()


def emit_top2_factor(p, c, logits_ap, f_ap, sc, dep):
    t1 = p.op("vector", lambda E: E.tensor_reduce(
        out=sc[:, 8:9], in_=logits_ap, axis=AX, op=ALU.max), deps=(dep,))
    t1b = p.op("vector", lambda E: E.tensor_scalar_mul(
        out=sc[:, 9:10], in0=sc[:, 8:9], scalar1=-1.0), deps=(t1,))
    t2 = p.op("scalar", lambda E: E.activation(
        out=sc[:, 0:8], in_=logits_ap, func=AF.Exp,
        bias=sc[:, 9:10], accum_out=sc[:, 10:11]), deps=(t1b,))
    t3 = p.op("vector", lambda E: E.reciprocal(
        out=sc[:, 11:12], in_=sc[:, 10:11]), deps=(t2,))
    t4 = p.op("vector", lambda E: E.tensor_scalar_mul(
        out=sc[:, 0:8], in0=sc[:, 0:8], scalar1=sc[:, 11:12]), deps=(t3,))
    t5 = p.op("vector", lambda E: E.tensor_reduce(
        out=sc[:, 12:13], in_=sc[:, 0:8], axis=AX, op=ALU.max), deps=(t4,))
    te = p.op("vector", lambda E: E.tensor_scalar(
        out=sc[:, 16:24], in0=sc[:, 0:8], scalar1=sc[:, 12:13],
        op0=ALU.is_ge, scalar2=None), deps=(t5,))
    tm = p.op("vector", lambda E: E.scalar_tensor_tensor(
        out=sc[:, 24:32], in0=sc[:, 16:24], scalar=-2.0,
        in1=sc[:, 0:8], op0=ALU.mult, op1=ALU.add), deps=(te,))
    tn = p.op("vector", lambda E: E.tensor_reduce(
        out=sc[:, 13:14], in_=sc[:, 24:32], axis=AX, op=ALU.max), deps=(tm,))
    ta = p.op("vector", lambda E: E.tensor_tensor(
        out=sc[:, 14:15], in0=sc[:, 12:13], in1=sc[:, 13:14], op=ALU.add),
        deps=(tn,))
    tb = p.op("vector", lambda E: E.tensor_scalar_max(
        out=sc[:, 15:16], in0=sc[:, 14:15], scalar1=1.1920929e-07), deps=(ta,))
    tc = p.op("vector", lambda E: E.reciprocal(
        out=sc[:, 15:16], in_=sc[:, 15:16]), deps=(tb,))
    return p.op("vector", lambda E: E.tensor_scalar(
        out=f_ap, in0=sc[:, 14:15], scalar1=sc[:, 15:16], op0=ALU.mult,
        scalar2=None), deps=(tc,))



LAYER_EMITTERS = {}


def build_program(layers, host_arrs):
    nc = bass.Bass("TRN2", target_bir_lowering=False)
    x_in = nc.dram_tensor("x_in", [T, D], F32, kind="ExternalInput")
    x_out = nc.dram_tensor("x_out", [T, D], F32, kind="ExternalOutput")
    if 0 in layers:
        W_extra = {"x_full": nc.dram_tensor("x_full", [8 * T, D], F32,
                                            kind="ExternalInput")}
    else:
        W_extra = {}
    W = dict(W_extra)
    for name, arr in host_arrs.items():
        W[name] = nc.dram_tensor(name, list(arr.shape), F32, kind="ExternalInput")

    with contextlib.ExitStack() as stack:
        p = Prog(nc, stack)
        c = Core(p, stack)
        c.init_consts(p, x_in)
        for li in layers:
            with contextlib.ExitStack() as lst:
                if li == 0:
                    emit_layer0(p, c, W, lst)
                elif li == 1:
                    emit_layer1(p, c, W, lst)
                elif li == 2:
                    emit_layer2(p, c, W, lst)
                elif li == 3:
                    emit_layer3(p, c, W, lst)
                else:
                    raise NotImplementedError(li)
            p.barrier()
        outr = x_out.rearrange("(n p) d -> n p d", p=128)
        for ti in range(NT):
            p.dma("sync", outr[ti], c.x[:, ti, :])
        p.barrier()
        p.emit()
    return nc


def kernel(**inputs):
    host = prep_host(inputs)
    xcur = np.ascontiguousarray(np.asarray(inputs["L0_attn_x"], np.float32))
    for li in (0, 1, 2, 3):
        nc = build_program([li], host)
        in_maps = []
        for core in range(NCORES):
            m = {"x_in": np.ascontiguousarray(xcur[core])}
            if li == 0:
                m["x_full"] = xcur.reshape(-1, D)
            m.update(host)
            in_maps.append(m)
        res = run_bass_kernel_spmd(nc, in_maps, core_ids=list(range(NCORES)))
        xcur = np.stack([res.results[i]["x_out"] for i in range(NCORES)], axis=0)
    return xcur
